# version 42
# baseline (speedup 1.0000x reference)
import numpy as np
import ml_dtypes
from contextlib import ExitStack

import concourse.bass as bass
import concourse.mybir as mybir
from concourse.bass_utils import run_bass_kernel_spmd

F32 = mybir.dt.float32
BF16 = mybir.dt.bfloat16
I32 = mybir.dt.int32
AF = mybir.ActivationFunctionType
ALU = mybir.AluOpType
AX = mybir.AxisListType
BFNP = ml_dtypes.bfloat16

N_CORES = 8


def _perm(d):
    pi = np.empty(d, np.int64)
    ar = np.arange(128)
    for t in range(d // 256):
        b = 256 * t
        pi[b:b + 128] = b + 2 * ar
        pi[b + 128:b + 256] = b + 1 + 2 * ar
    return pi


class Sched:
    ENGINES = ("sync", "gpsimd", "vector", "scalar", "tensor")

    def __init__(self):
        self.ev = {e: [] for e in self.ENGINES}
        self.cnt = {}
        self.state = {}

    def plan(self, sem, ahead=1):
        return (sem, self.cnt.get(sem, 0) + ahead)

    def op(self, eng, fn, reads=(), writes=(), sem=None, inc=1, extra=(),
           future=None, record=None):
        sem = sem or ("S_" + eng)
        waits = {}

        def need(tok):
            if tok is None:
                return
            s, v = tok
            if waits.get(s, 0) < v:
                waits[s] = v

        for r in reads:
            st = self.state.get(r)
            if st:
                need(st[0])
        for wr in writes:
            st = self.state.get(wr)
            if st:
                need(st[0])
                for s, v in st[1].items():
                    need((s, v))
        for tok in extra:
            need(tok)
        if inc:
            newv = self.cnt.get(sem, 0) + inc
            self.cnt[sem] = newv
            tok = (sem, newv)
        else:
            tok = future
        if record is not None:
            tok = record
        self.ev[eng].append((waits, fn, (sem, inc) if inc else None))
        for r in reads:
            st = self.state.setdefault(r, [None, {}])
            if tok is not None:
                s, v = tok
                st[1][s] = max(st[1].get(s, 0), v)
        for wr in writes:
            self.state[wr] = [tok, {}]
        return tok

    def emit_engine(self, e, obj, sems):
        hw = {}
        for waits, fn, inc in self.ev[e]:
            for s in sorted(waits):
                v = waits[s]
                if hw.get(s, 0) >= v:
                    continue
                obj.wait_ge(sems[s], v)
                hw[s] = v
            ins = fn(obj)
            if inc is not None:
                s, n = inc
                ins.then_inc(sems[s], n)


def build_program(D, SC):
    C = D // 128
    NSL = D // 256
    XG = C // 4
    SHW = min(512, SC)
    NH = SC // SHW
    KBN = D // 32
    KBT = max(32, KBN)
    EMC = (C // 2) * KBN
    assert SC % SHW == 0 and NH in (1, 2)

    nc = bass.Bass("TRN2", target_bir_lowering=False, debug=False,
                   dynamic_dma_scratch_size=16384)

    t_xT = nc.dram_tensor("xT", [D, SC], F32, kind="ExternalInput")
    t_wt = nc.dram_tensor("wtpr", [C, 128, C, 128], F32, kind="ExternalInput")
    t_mask = nc.dram_tensor("maskn", [D, D], I32, kind="ExternalInput")
    t_pt = nc.dram_tensor("ptpr", [C, 128, C // 2, 128], I32, kind="ExternalInput")
    t_zp2 = nc.dram_tensor("zp2", [128, C], F32, kind="ExternalInput")
    t_sc2 = nc.dram_tensor("sc2", [128, C], F32, kind="ExternalInput")
    t_bi2 = nc.dram_tensor("bi2", [128, C], F32, kind="ExternalInput")
    t_g8a = nc.dram_tensor("g8a", [128, 8], BF16, kind="ExternalInput")
    t_g8b = nc.dram_tensor("g8b", [128, 8], BF16, kind="ExternalInput")
    t_sel = nc.dram_tensor("sel", [32, 512], BF16, kind="ExternalInput")
    t_out = nc.dram_tensor("outT", [D, SC], F32, kind="ExternalOutput")
    t_bmd = nc.dram_tensor("bm_scr", [KBN, KBN], BF16, kind="Internal")
    t_bmx = nc.dram_tensor("bmx_scr", [16 * KBN, KBN], BF16, kind="Internal")
    t_rowd = nc.dram_tensor("row_scr", [1, 2 * SC], F32, kind="Internal")
    t_rsd = nc.dram_tensor("rs_scr", [1, SC], F32, kind="Internal")

    sch = Sched()
    P = sch.op

    with ExitStack() as st_:
        E = st_.enter_context
        arena_a = E(nc.sbuf_tensor("arena_a", [128, C * SC], BF16))
        arena_b = E(nc.sbuf_tensor("arena_b", [128, C * SC], BF16))
        scr_i32 = E(nc.sbuf_tensor("scr_i32", [128, 2 * D], I32))
        mskv = scr_i32.bitcast(BF16)
        scr_bf = E(nc.sbuf_tensor("scr_bf", [128, 4 * D], BF16))
        em = E(nc.sbuf_tensor("em", [128, EMC], BF16))
        crbf = E(nc.sbuf_tensor("crbf", [128, 4 * KBN], BF16))
        bmb = E(nc.sbuf_tensor("bmb", [32, KBT], BF16))
        bmtr = E(nc.sbuf_tensor("bmtr", [32, KBT], BF16))
        onesc = E(nc.sbuf_tensor("onesc", [128, 1], BF16))
        g8a = E(nc.sbuf_tensor("g8a_s", [128, 8], BF16))
        g8b = E(nc.sbuf_tensor("g8b_s", [128, 8], BF16))
        sel_s = E(nc.sbuf_tensor("sel_s", [32, 512], BF16))
        zp2n = E(nc.sbuf_tensor("zp2n", [128, C], BF16))
        sc2s = E(nc.sbuf_tensor("sc2s", [128, C], F32))
        bi2s = E(nc.sbuf_tensor("bi2s", [128, C], BF16))
        mb_b = E(nc.sbuf_tensor("mb_b", [128, SC], BF16))
        rb_b = E(nc.sbuf_tensor("rb_b", [128, SC], F32))
        rsb = E(nc.sbuf_tensor("rsb", [128, SC], BF16))
        sqst = mb_b
        if EMC * 2 == SC * 4:
            stage = em.bitcast(F32)
            stage_b = em
        else:
            stage = E(nc.sbuf_tensor("stage", [128, SC], F32))
            stage_b = stage.bitcast(BF16)
        if 7 * SC + 256 <= 2 * D:
            fout = scr_bf.bitcast(F32)
            rows = fout
            RB = 5 * SC

            onesr_ap = fout[0:1, RB + 2 * SC: RB + 2 * SC + 128]
            epsc_ap = fout[0:1, RB + 2 * SC + 128: RB + 2 * SC + 129]
            LATE_CONST = True
        else:
            fout = E(nc.sbuf_tensor("fout", [128, 2 * SC], F32))
            rowsmall = E(nc.sbuf_tensor("rowsmall", [1, 2 * SC], F32))
            rows = rowsmall
            RB = 0
            onesr_t = E(nc.sbuf_tensor("onesr", [1, 128], F32))
            epsc_t = E(nc.sbuf_tensor("epsc", [1, 1], F32))
            onesr_ap = onesr_t[0:1, :]
            epsc_ap = epsc_t[0:1, :]
            LATE_CONST = False
        ps = [E(nc.psum_tensor("ps%d" % i, [128, SHW], F32)) for i in range(8)]
        sems = {k: E(nc.semaphore(k)) for k in
                ("S_sync", "S_gpsimd", "S_vector", "S_scalar", "S_tensor",
                 "S_taux", "Q_sm", "Q_sg", "Q_b", "Q_r",
                 "Q_x0", "Q_x1", "Q_x2", "Q_x3",
                 "Qm0", "Qm1", "Qm2", "Qm3", "Qsc0", "Qsc1", "Qem0", "Qem1", "Qem2", "Qem3", "Qw0", "Qw1", "Qw2", "Qw3", "Qq0", "Qq1", "Qo0", "Qo1")}
        blk = E(nc.Block())

        def dma(eng, qsem, out_ap, in_ap, reads=(), writes=(), record=None,
                selfwait=False):
            tok = P(eng, lambda o, a=out_ap, b=in_ap: o.dma_start(out=a, in_=b),
                    reads=reads, writes=writes, sem=qsem, inc=16, record=record)
            if selfwait:
                P(eng, lambda o: o.nop(), extra=[(qsem, sch.cnt[qsem])], inc=0)
            return tok

        def bank_data(vw, h):
            return (NH * vw + h) % 3

        FFNB = (0, 1, 2, 7)

        def bank_ffn(w, h):
            return FFNB[(NH * w + h) % 4]

        smtot = ("Q_sm", 4 * 16)
        sgtot = ("Q_sg", 2 * 16)
        dma("gpsimd", "Q_sg", zp2n[:], t_zp2.ap(), writes=[("zp2s",)], record=sgtot)
        dma("sync", "Q_sm", sc2s[:], t_sc2.ap(), writes=[("sc2s",)], record=smtot)
        dma("gpsimd", "Q_sg", bi2s[:], t_bi2.ap(), writes=[("bi2s",)], record=sgtot)
        dma("sync", "Q_sm", g8a[:], t_g8a.ap(), writes=[("g8a",)], record=smtot)
        dma("sync", "Q_sm", g8b[:], t_g8b.ap(), writes=[("g8b",)], record=smtot)
        dma("sync", "Q_sm", sel_s[:], t_sel.ap(), writes=[("sel",)], record=smtot)
        P("vector", lambda o: o.memset(onesc[:], 1.0), writes=[("ones",)])
        if not LATE_CONST:
            P("vector", lambda o: o.memset(onesr_ap, 1.0), writes=[("onesr",)])
            P("vector", lambda o: o.memset(epsc_ap, 1e-5), writes=[("epsc",)])
        P("vector", lambda o: o.memset(bmb[:], 0.0), writes=[("bmb",)])
        P("vector", lambda o: o.memset(bmtr[:], 0.0), writes=[("bmtr",)])
        P("vector", lambda o: o.tensor_scalar(zp2n[:], zp2n[:], -1.0, None, ALU.mult),
          reads=[("zp2s",)], writes=[("zp2n",)])

        NXD = min(4, XG)
        XDIV = C // NXD
        xT_r = t_xT.ap().rearrange("(g i p) s -> g p i s", i=XDIV, p=128)

        def xq_load(g):
            out_ap = arena_a[:, XDIV * SC * g: XDIV * SC * (g + 1)].rearrange(
                "p (i s) -> p i s", s=SC)
            dma("gpsimd", "Q_x%d" % g, out_ap, xT_r[g], writes=[("xq", g)])

        def mask_stage_a(m):
            for half in range(2):
                i = 2 * m + half
                slot = i % 4
                cslot = i % 4
                msl = mskv[:, D * slot: D * (slot + 1)]
                dma("gpsimd", "Qm%d" % slot, msl,
                    t_mask.ap()[128 * i: 128 * (i + 1), :],
                    writes=[("msl", slot)])
                def red_fn(o, d=cslot, sl=slot):
                    mv = bass.AP(mskv, D * sl, [[4 * D, 128], [1, KBN], [KBN, 32]])
                    with nc.allow_low_precision(reason="int 0/1 sums <= 1024 exact in bf16"):
                        return o.tensor_reduce(crbf[:, KBN * d: KBN * (d + 1)],
                                               mv, AX.X, ALU.add)
                P("vector", red_fn, reads=[("msl", slot)], writes=[("crb", cslot)])

        def mask_stage_b(m):
            for half in range(2):
                i = 2 * m + half
                cslot = i % 4
                gg = g8a if half == 0 else g8b
                ggr = ("g8a",) if half == 0 else ("g8b",)
                P("tensor",
                  lambda o, d=cslot, g_=gg, h=half: o.matmul(
                      ps[7][0:8, 0:KBN], g_[:, 0:8],
                      crbf[:, KBN * d: KBN * (d + 1)],
                      start=(h == 0), stop=(h == 1), skip_group_check=True),
                  reads=[("crb", cslot), ggr], writes=[("PB", 7)], sem="S_taux")
            P("vector",
              lambda o: o.tensor_scalar(bmb[0:8, 0:KBN], ps[7][0:8, 0:KBN],
                                        0.0, None, ALU.is_gt),
              reads=[("PB", 7)], writes=[("bmb",)])
            P("vector", lambda o: o.transpose(bmtr[:, 0:KBT], bmb[:, 0:KBT]),
              reads=[("bmb",)], writes=[("bmtr",)])
            rcnt = min(32, KBN)
            jcnt = (KBN + 31) // 32
            qs = "Qsc%d" % (m % 2)
            sct = (qs, sch.cnt.get(qs, 0) + 16 * jcnt)
            for j in range(jcnt):
                in_ap = bass.AP(bmtr, 32 * j, [[KBT, rcnt], [0, 16], [1, 8]])
                out_ap = bass.AP(t_bmx, (16 * 32 * j) * KBN + 8 * m,
                                 [[16 * KBN, rcnt], [KBN, 16], [1, 8]])
                dma("sync", qs, out_ap, in_ap, reads=[("bmtr",)],
                    writes=([("bmx", m)] if j == 0 else []), record=sct)
            in2 = bass.AP(t_bmx, 8 * m, [[KBN, 128], [128 * KBN, C // 2], [1, 8]])
            out2 = bass.AP(em, 8 * m, [[EMC, 128], [KBN, C // 2], [1, 8]])
            dma("sync", "Qem%d" % (m % 4), out2, in2, reads=[("bmx", m)],
                writes=[("EM", m)])

        def wt_refill(v):
            slot = v % 4
            out_ap = scr_bf[:, D * slot: D * (slot + 1)]
            in_ap = t_wt.ap()[v].rearrange("k c n -> k (c n)")
            dma("gpsimd", "Qw%d" % slot, out_ap, in_ap,
                writes=[("wts", slot), ("wts", slot, 0), ("wts", slot, 1)])

        def wt_mult(v):
            slot = v % 4
            off = D * slot
            for par in range(2):
                in1 = bass.AP(em, 8 * (v // 2),
                              [[EMC, 128], [KBN, C // 2], [1, 8], [0, 16]])
                io = bass.AP(scr_bf, off + 128 * par,
                             [[4 * D, 128], [256, C // 2], [16, 8], [1, 16]])
                P("vector" if par == 0 else "gpsimd",
                  lambda o, a=io, b=in1: o.tensor_tensor(a, a, b, ALU.mult),
                  reads=[("wts", slot), ("EM", v // 2)],
                  writes=[("wts", slot, par)])

        wt_refill(0)
        mask_stage_a(0)
        xq_load(0)
        if C > 1:
            wt_refill(1)
        mask_stage_b(0)
        wt_mult(0)
        if NSL > 1:
            mask_stage_a(1)
        if C > 2:
            wt_refill(2)
        for g_ in range(1, NXD):
            xq_load(g_)

        def stats_mms(vv, src, s1_regions, sq_region, first, last):
            for h in range(NH):
                P("tensor",
                  lambda o, v_=vv, h_=h, s_=src: o.matmul(
                      ps[3 + h_][0:1, 0:SHW], onesc[:, 0:1],
                      s_[:, SC * v_ + SHW * h_: SC * v_ + SHW * (h_ + 1)],
                      start=first, stop=last, skip_group_check=True),
                  reads=[s1_regions(vv, h), ("ones",)], writes=[("PB", 3 + h)],
                  sem="S_taux")
            for h in range(NH):
                P("tensor",
                  lambda o, h_=h, v_=vv: o.matmul(
                      ps[5 + h_][0:1, 0:SHW], onesc[:, 0:1],
                      sqst[:, SHW * h_: SHW * (h_ + 1)],
                      start=first, stop=last, skip_group_check=True),
                  reads=[sq_region, ("ones",)], writes=[("PB", 5 + h)],
                  sem="S_taux")

        for v in range(C):
            if v + 1 < C:
                wt_mult(v + 1)
            if v % 2 == 1:
                if (v + 3) // 2 < NSL:
                    mask_stage_b((v + 3) // 2)
            corder = list(range(0, C, 2)) + list(range(1, C, 2))
            for h in range(NH):
                bk = bank_data(v, h)
                fut = sch.plan("S_tensor")
                for ci, c in enumerate(corder):
                    P("tensor",
                      lambda o, v_=v, h_=h, c_=c, b_=bk, st=(ci == 0),
                      sp=(ci == C - 1): o.matmul(
                          ps[b_][:, 0:SHW],
                          scr_bf[:, D * (v_ % 4) + 128 * c_: D * (v_ % 4) + 128 * (c_ + 1)],
                          arena_a[:, SC * c_ + SHW * h_: SC * c_ + SHW * (h_ + 1)],
                          start=st, stop=sp,
                          skip_group_check=True),
                      reads=[("wts", v % 4, c % 2), ("xq", c // XDIV)],
                      writes=([("PB", bk)] if ci == 0 else []),
                      inc=(1 if ci == C - 1 else 0), future=fut)
            if v >= 1:
                stats_mms(v - 1, arena_b, lambda vv, hh: ("h", vv, hh), ("sq", 0),
                          v - 1 == 0, v - 1 == C - 1)
            for h in range(NH):
                bk = bank_data(v, h)
                P("vector",
                  lambda o, v_=v, h_=h, b_=bk: o.tensor_tensor(
                      arena_b[:, SC * v_ + SHW * h_: SC * v_ + SHW * (h_ + 1)],
                      ps[b_][:, 0:SHW],
                      arena_a[:, SC * v_ + SHW * h_: SC * v_ + SHW * (h_ + 1)],
                      ALU.add),
                  reads=[("PB", bk), ("xq", v // XDIV)], writes=[("h", v, h)])
            P("scalar",
              lambda o, v_=v: o.activation(sqst[:, 0:SC],
                                           arena_b[:, SC * v_: SC * (v_ + 1)],
                                           AF.Square),
              reads=[("h", v, 0), ("h", v, NH - 1)], writes=[("sq", 0)])
            if v == 0 and NSL > 2:
                mask_stage_a(2)
            if v == 0 and NSL > 1:
                mask_stage_b(1)
            if v % 2 == 0 and v // 2 + 3 < NSL:
                mask_stage_a(v // 2 + 3)
            if v + 3 < C:
                wt_refill(v + 3)
        stats_mms(C - 1, arena_b, lambda vv, hh: ("h", vv, hh), ("sq", 0),
                  C - 1 == 0, True)

        def ln_finalize(mrb_tag):
            for h in range(NH):
                P("vector",
                  lambda o, h_=h: o.tensor_scalar(
                      rows[0:1, RB + SHW * h_: RB + SHW * (h_ + 1)], ps[3 + h_][0:1, 0:SHW],
                      1.0 / D, None, ALU.mult),
                  reads=[("PB", 3 + h)],
                  writes=[("rows_m", h), (mrb_tag + "_m",)])
            for seg in range(NH):
                pb = FFNB[seg]
                P("tensor",
                  lambda o, sg=seg, p_=pb: o.matmul(
                      ps[p_][:, 0:SHW], onesr_ap,
                      rows[0:1, RB + SHW * sg: RB + SHW * (sg + 1)],
                      start=True, stop=True, skip_group_check=True),
                  reads=[("rows_m", 0), ("rows_m", NH - 1), ("onesr",)],
                  writes=[("PB", pb)], sem="S_taux")
                P("scalar",
                  lambda o, sg=seg, p_=pb: o.copy(
                      mb_b[:, SHW * sg: SHW * (sg + 1)], ps[p_][:, 0:SHW]),
                  reads=[("PB", pb)],
                  writes=[(mrb_tag + "_m",), ("rows_m", seg), ("sq", 0)])
            for h in range(NH):
                P("vector",
                  lambda o, h_=h: o.tensor_scalar(
                      rows[0:1, RB + SC + SHW * h_: RB + SC + SHW * (h_ + 1)],
                      ps[5 + h_][0:1, 0:SHW], 1.0 / D, None, ALU.mult),
                  reads=[("PB", 5 + h)],
                  writes=[("rows_s", h), (mrb_tag + "_r",)])
            if 2 * SC <= D:
                msq = fout[0:1, 4 * SC: 5 * SC]
                msq_w = [("pst",)]
                msq_r = [("pst",)]
            else:
                msq = stage[0:1, 0:SC]
                msq_w = [("pst",), ("stage", 0), ("stage", NH - 1)]
                msq_r = list(msq_w)
            P("vector",
              lambda o: o.tensor_tensor(msq, rows[0:1, RB: RB + SC],
                                        rows[0:1, RB: RB + SC], ALU.mult),
              reads=[("rows_m", 0), ("rows_m", NH - 1)], writes=msq_w)
            P("vector",
              lambda o: o.tensor_tensor(rows[0:1, RB + SC: RB + 2 * SC],
                                        rows[0:1, RB + SC: RB + 2 * SC], msq,
                                        ALU.subtract),
              reads=[("rows_s", 0), ("rows_s", NH - 1)] + msq_r,
              writes=[("rows_s", 0), ("rows_s", NH - 1)])
            P("scalar",
              lambda o: o.activation(rows[0:1, RB + SC: RB + 2 * SC],
                                     rows[0:1, RB + SC: RB + 2 * SC],
                                     AF.Ln, bias=epsc_ap),
              reads=[("rows_s", 0), ("epsc",)],
              writes=[("rows_s", 0), ("rows_s", NH - 1)])
            P("scalar",
              lambda o: o.activation(rows[0:1, RB + SC: RB + 2 * SC],
                                     rows[0:1, RB + SC: RB + 2 * SC],
                                     AF.Exp, scale=-0.5),
              reads=[("rows_s", 0)], writes=[("rows_s", 0), ("rows_s", NH - 1)])
            for seg in range(NH):
                pb = FFNB[2 + seg]
                P("tensor",
                  lambda o, sg=NH + seg, p_=pb: o.matmul(
                      ps[p_][:, 0:SHW], onesr_ap,
                      rows[0:1, RB + SHW * sg: RB + SHW * (sg + 1)],
                      start=True, stop=True, skip_group_check=True),
                  reads=[("rows_s", 0), ("rows_s", NH - 1), ("onesr",)],
                  writes=[("PB", pb)], sem="S_taux")
                P("scalar",
                  lambda o, sg=seg, p_=pb: o.copy(
                      rb_b[:, SHW * sg: SHW * (sg + 1)], ps[p_][:, 0:SHW]),
                  reads=[("PB", pb)],
                  writes=[(mrb_tag + "_r",), ("rows_s", seg)])

        if LATE_CONST:
            P("vector", lambda o: o.memset(onesr_ap, 1.0),
              writes=[("onesr",), ("wts", 3), ("wts", 3, 0), ("wts", 3, 1)])
            P("vector", lambda o: o.memset(epsc_ap, 1e-5), writes=[("epsc",)])
        ln_finalize("mrb")

        def q_dma(w):
            slot = w % 2
            q0 = (D // 2) * slot
            out_ap = scr_i32[:, q0: q0 + D // 2]
            dma("scalar", "Qq%d" % slot, out_ap,
                t_pt.ap()[w].rearrange("j jt b -> j (jt b)"),
                writes=[("qsl", slot), ("msl", slot)])

        def unpack(w):
            slot = w % 2
            q0 = (D // 2) * slot
            qs = scr_i32[:, q0: q0 + D // 2]
            P("vector",
              lambda o, q=qs: o.tensor_scalar(scr_i32[:, D: D + D // 2], q, 15, None,
                                              ALU.bitwise_and),
              reads=[("qsl", slot)], writes=[("l32",), ("msl", 2)])
            P("vector",
              lambda o, q=qs: o.tensor_scalar(scr_i32[:, D + D // 2: 2 * D], q, 4, 15,
                                              ALU.logical_shift_right, ALU.bitwise_and),
              reads=[("qsl", slot)], writes=[("h32",), ("msl", 3)])
            P("scalar",
              lambda o, s=slot: o.copy(scr_bf[:, D * s: D * s + D // 2],
                                       scr_i32[:, D: D + D // 2]),
              reads=[("l32",)], writes=[("lobf", slot), ("wts", slot)])
            P("scalar",
              lambda o, s=slot: o.copy(scr_bf[:, D * s + D // 2: D * (s + 1)],
                                       scr_i32[:, D + D // 2: 2 * D]),
              reads=[("h32",)], writes=[("hibf", slot), ("wts", slot)])

        def ffn_mm(w, h, c, first, last, fut):
            slot = w % 2
            jt = c // 2
            par = c % 2
            woff = D * slot + (D // 2) * par + 128 * jt
            bk = bank_ffn(w, h)
            return P("tensor",
                     lambda o, w_=w, h_=h, c_=c, b_=bk, wo=woff: o.matmul(
                         ps[b_][:, 0:SHW], scr_bf[:, wo: wo + 128],
                         arena_b[:, SC * c_ + SHW * h_: SC * c_ + SHW * (h_ + 1)],
                         start=first, stop=last, skip_group_check=True),
                     reads=[("lobf", slot) if par == 0 else ("hibf", slot),
                            ("hT", c)],
                     writes=([("PB", bk)] if first else []),
                     inc=(1 if last else 0), future=fut)

        def epilogue(w):
            for h in range(NH):
                bk = bank_ffn(w, h)
                P("vector",
                  lambda o, w_=w, h_=h, b_=bk: o.scalar_tensor_tensor(
                      stage[:, SHW * h_: SHW * (h_ + 1)],
                      rsb[:, SHW * h_: SHW * (h_ + 1)], zp2n[:, w_: w_ + 1],
                      ps[b_][:, 0:SHW], ALU.mult, ALU.add),
                  reads=[("PB", bk), ("rsb",), ("zp2n",)], writes=[("stage", h)])
                P("scalar",
                  lambda o, w_=w, h_=h: o.activation(
                      stage[:, SHW * h_: SHW * (h_ + 1)],
                      stage[:, SHW * h_: SHW * (h_ + 1)], AF.Identity,
                      bias=bi2s[:, w_: w_ + 1], scale=sc2s[:, w_: w_ + 1]),
                  reads=[("stage", h), ("bi2s",), ("sc2s",)], writes=[("stage", h)])
                P("vector",
                  lambda o, w_=w, h_=h: o.tensor_tensor(
                      arena_a[:, SC * w_ + SHW * h_: SC * w_ + SHW * (h_ + 1)],
                      stage[:, SHW * h_: SHW * (h_ + 1)],
                      arena_b[:, SC * w_ + SHW * h_: SC * w_ + SHW * (h_ + 1)],
                      ALU.add),
                  reads=[("stage", h), ("hT", w)],
                  writes=[("x2", w, h), ("xq", w // XDIV)])
            P("scalar",
              lambda o, w_=w: o.activation(sqst[:, 0:SC],
                                           arena_a[:, SC * w_: SC * (w_ + 1)],
                                           AF.Square),
              reads=[("x2", w, 0), ("x2", w, NH - 1)],
              writes=[("sq", 0), ("mrb_m",)])

        q_dma(0)
        if C > 1:
            q_dma(1)
        unpack(0)
        futs0 = [sch.plan("S_tensor", ahead=1 + h) for h in range(NH)]
        for c in range(C):
            eng = "gpsimd" if (c % 3 == 2 and C > 4) else "vector"
            if eng == "vector":
                stg = stage[:, 0:SC]
                stgr = [("stage", 0), ("stage", NH - 1)]
            else:
                stg = fout[:, 4 * SC: 5 * SC]
                stgr = [("pst",)]
            P(eng,
              lambda o, c_=c, sg=stg: o.tensor_tensor(
                  sg, arena_b[:, SC * c_: SC * (c_ + 1)],
                  mb_b[:], ALU.subtract),
              reads=[("h", c, 0), ("h", c, NH - 1), ("mrb_m",)],
              writes=stgr +
                     ([("EM", m) for m in range(NSL)] if c == 0 else []) +
                     ([("wts", 2), ("wts", 2, 0), ("wts", 2, 1)]
                      if eng == "gpsimd" and c == 3 else []))
            P(eng,
              lambda o, c_=c, sg=stg: o.tensor_tensor(
                  arena_b[:, SC * c_: SC * (c_ + 1)],
                  sg, rb_b[:], ALU.mult),
              reads=stgr + [("mrb_r",)],
              writes=[("hT", c), ("h", c, 0), ("h", c, NH - 1)])
            for h in range(NH):
                ffn_mm(0, h, c, c == 0, c == C - 1, futs0[h])
            for h in range(NH):
                P("tensor",
                  lambda o, c_=c, h_=h: o.matmul(
                      ps[3 + h_][0:1, 0:SHW], onesc[:, 0:1],
                      arena_b[:, SC * c_ + SHW * h_: SC * c_ + SHW * (h_ + 1)],
                      start=(c_ == 0), stop=(c_ == C - 1), skip_group_check=True),
                  reads=[("hT", c), ("ones",)], writes=[("PB", 3 + h)],
                  sem="S_taux")
        for h in range(NH):
            P("vector",
              lambda o, h_=h: o.tensor_copy(rows[0:1, RB + SHW * h_: RB + SHW * (h_ + 1)],
                                            ps[3 + h_][0:1, 0:SHW]),
              reads=[("PB", 3 + h)], writes=[("rows_m", h), ("mrb",)])
        for seg in range(NH):
            pb = (2, 7)[seg % 2]
            P("tensor",
              lambda o, sg=seg, p_=pb: o.matmul(
                  ps[p_][:, 0:SHW], onesr_ap,
                  rows[0:1, RB + SHW * sg: RB + SHW * (sg + 1)],
                  start=True, stop=True, skip_group_check=True),
              reads=[("rows_m", 0), ("rows_m", NH - 1), ("onesr",)],
              writes=[("PB", pb)], sem="S_taux")
            P("scalar",
              lambda o, sg=seg, p_=pb: o.copy(
                  rsb[:, SHW * sg: SHW * (sg + 1)], ps[p_][:, 0:SHW]),
              reads=[("PB", pb)], writes=[("rsb",)])
        unpack(1)
        epilogue(0)

        for w in range(1, C):
            if w + 1 < C:
                q_dma(w + 1)
            if w + 1 < C:
                unpack(w + 1)
            corder = list(range(0, C, 2)) + list(range(1, C, 2))
            for h in range(NH):
                fut = sch.plan("S_tensor")
                for ci, c in enumerate(corder):
                    ffn_mm(w, h, c, ci == 0, ci == C - 1, fut)
            stats_mms(w - 1, arena_a, lambda vv, hh: ("x2", vv, hh), ("sq", 0),
                      w - 1 == 0, w - 1 == C - 1)
            epilogue(w)
        stats_mms(C - 1, arena_a, lambda vv, hh: ("x2", vv, hh), ("sq", 0),
                  C - 1 == 0, True)

        ln_finalize("mrb2")
        for c in range(C):
            eng = "gpsimd" if ((3 * c) % 7 < 3 and C > 4) else "vector"
            if eng == "vector":
                stg = stage[:, 0:SC]
                stgr = [("stage", 0), ("stage", NH - 1)]
            else:
                stg = fout[:, 4 * SC: 5 * SC]
                stgr = [("pst",)]
            P(eng,
              lambda o, c_=c, sg=stg: o.tensor_tensor(
                  sg, arena_a[:, SC * c_: SC * (c_ + 1)],
                  mb_b[:], ALU.subtract),
              reads=[("x2", c, 0), ("x2", c, NH - 1), ("mrb2_m",)],
              writes=stgr)
            P(eng,
              lambda o, c_=c, sg=stg: o.tensor_tensor(
                  fout[:, SC * (c_ % 2): SC * (c_ % 2 + 1)],
                  sg, rb_b[:], ALU.mult),
              reads=stgr + [("mrb2_r",)],
              writes=[("fo", c % 2), ("lobf", 0) if c % 2 == 0 else ("hibf", 0)])
            dma("scalar", "Qo%d" % (c % 2), t_out.ap()[128 * c: 128 * (c + 1), :],
                fout[:, SC * (c % 2): SC * (c % 2 + 1)],
                reads=[("fo", c % 2)], writes=[("outd", c)])
        P("sync", lambda o: o.nop(),
          extra=[("Qo0", sch.cnt.get("Qo0", 0)), ("Qo1", sch.cnt.get("Qo1", 0))],
          inc=0)

        @blk.sync
        def _(e):
            sch.emit_engine("sync", e, sems)

        @blk.gpsimd
        def _(e):
            sch.emit_engine("gpsimd", e, sems)

        @blk.vector
        def _(e):
            sch.emit_engine("vector", e, sems)

        @blk.scalar
        def _(e):
            sch.emit_engine("scalar", e, sems)

        @blk.tensor
        def _(e):
            sch.emit_engine("tensor", e, sems)

    nc._t_out = t_out
    return nc


def host_prepare(D, SC_total, x, W, mask, packed, scale, zp, bias, n_cores):
    pi = _perm(D)
    C = D // 128
    T = x.shape[0]
    SC = T // n_cores
    xT = np.ascontiguousarray(x.T)[pi]
    wtp = W[np.ix_(pi, pi)].T
    wtpr = np.ascontiguousarray(
        wtp.reshape(C, 128, C, 128).transpose(2, 1, 0, 3))
    P = packed.reshape(D, D // 2)
    ptp = P[pi].T
    ptpr = np.ascontiguousarray(
        ptp.reshape(C // 2, 128, C, 128).transpose(2, 1, 0, 3))
    zp2 = np.ascontiguousarray(zp.reshape(D)[pi].reshape(C, 128).T)
    sc2 = np.ascontiguousarray(scale.reshape(D)[pi].reshape(C, 128).T)
    bi2 = np.ascontiguousarray(bias.reshape(D)[pi].reshape(C, 128).T)
    ar = np.arange(128)
    g8a = (ar[:, None] // 32 == np.arange(8)[None, :]).astype(BFNP)
    g8b = (ar[:, None] // 32 + 4 == np.arange(8)[None, :]).astype(BFNP)
    r32 = np.arange(32)
    sel = np.concatenate(
        [(r32[:, None] == 8 * q + ar[None, :] // 16) for q in range(4)],
        axis=1).astype(BFNP)
    kbn = D // 32
    maskn = np.ascontiguousarray(
        mask.astype(np.int32, copy=False).reshape(D, kbn, 32)
        .transpose(0, 2, 1).reshape(D, D))
    shared = dict(wtpr=wtpr.astype(np.float32, copy=False), maskn=maskn, sel=sel,
                  ptpr=ptpr.astype(np.int32, copy=False),
                  zp2=zp2.astype(np.float32, copy=False),
                  sc2=sc2.astype(np.float32, copy=False),
                  bi2=bi2.astype(np.float32, copy=False), g8a=g8a, g8b=g8b)
    in_maps = []
    for cix in range(n_cores):
        m = dict(shared)
        m["xT"] = np.ascontiguousarray(xT[:, SC * cix: SC * (cix + 1)])
        in_maps.append(m)
    inv = np.empty(D, np.int64)
    inv[pi] = np.arange(D)
    return in_maps, inv, SC


def kernel(L0_attn_data, L0_attn_weight, L0_attn_mask, L0_ffn_weight_packed,
           L0_ffn_scale, L0_ffn_zero_point, L0_ffn_bias):
    D = L0_attn_weight.shape[0]
    B, S, _ = L0_attn_data.shape
    x = np.ascontiguousarray(np.asarray(L0_attn_data, dtype=np.float32).reshape(B * S, D))
    in_maps, inv, SC = host_prepare(
        D, B * S, x, np.asarray(L0_attn_weight, np.float32),
        np.asarray(L0_attn_mask), np.asarray(L0_ffn_weight_packed, np.int32),
        np.asarray(L0_ffn_scale, np.float32), np.asarray(L0_ffn_zero_point, np.float32),
        np.asarray(L0_ffn_bias, np.float32), N_CORES)
    nc = build_program(D, SC)
    res = run_bass_kernel_spmd(nc, in_maps, core_ids=list(range(N_CORES)))
    out = np.empty((B * S, D), np.float32)
    for cix in range(N_CORES):
        outT = np.asarray(res.results[cix]["outT"])
        out[SC * cix: SC * (cix + 1)] = outT.T[:, inv]
    return out.reshape(B, S, D)


# version 43
# speedup vs baseline: 1.1242x; 1.1242x over previous
import numpy as np
import ml_dtypes
from contextlib import ExitStack

import concourse.bass as bass
import concourse.mybir as mybir
from concourse.bass_utils import run_bass_kernel_spmd

F32 = mybir.dt.float32
BF16 = mybir.dt.bfloat16
I32 = mybir.dt.int32
AF = mybir.ActivationFunctionType
ALU = mybir.AluOpType
AX = mybir.AxisListType
BFNP = ml_dtypes.bfloat16

N_CORES = 8


def _perm(d):
    pi = np.empty(d, np.int64)
    ar = np.arange(128)
    for t in range(d // 256):
        b = 256 * t
        pi[b:b + 128] = b + 2 * ar
        pi[b + 128:b + 256] = b + 1 + 2 * ar
    return pi


class Sched:
    ENGINES = ("sync", "gpsimd", "vector", "scalar", "tensor")

    def __init__(self):
        self.ev = {e: [] for e in self.ENGINES}
        self.cnt = {}
        self.state = {}

    def plan(self, sem, ahead=1):
        return (sem, self.cnt.get(sem, 0) + ahead)

    def op(self, eng, fn, reads=(), writes=(), sem=None, inc=1, extra=(),
           future=None, record=None):
        sem = sem or ("S_" + eng)
        waits = {}

        def need(tok):
            if tok is None:
                return
            s, v = tok
            if waits.get(s, 0) < v:
                waits[s] = v

        for r in reads:
            st = self.state.get(r)
            if st:
                need(st[0])
        for wr in writes:
            st = self.state.get(wr)
            if st:
                need(st[0])
                for s, v in st[1].items():
                    need((s, v))
        for tok in extra:
            need(tok)
        if inc:
            newv = self.cnt.get(sem, 0) + inc
            self.cnt[sem] = newv
            tok = (sem, newv)
        else:
            tok = future
        if record is not None:
            tok = record
        self.ev[eng].append((waits, fn, (sem, inc) if inc else None))
        for r in reads:
            st = self.state.setdefault(r, [None, {}])
            if tok is not None:
                s, v = tok
                st[1][s] = max(st[1].get(s, 0), v)
        for wr in writes:
            self.state[wr] = [tok, {}]
        return tok

    def emit_engine(self, e, obj, sems):
        hw = {}
        for waits, fn, inc in self.ev[e]:
            for s in sorted(waits):
                v = waits[s]
                if hw.get(s, 0) >= v:
                    continue
                obj.wait_ge(sems[s], v)
                hw[s] = v
            ins = fn(obj)
            if inc is not None:
                s, n = inc
                ins.then_inc(sems[s], n)


def build_program(D, SC):
    C = D // 128
    NSL = D // 256
    XG = C // 4
    SHW = min(512, SC)
    NH = SC // SHW
    KBN = D // 32
    KBT = max(32, KBN)
    EMC = (C // 2) * KBN
    assert SC % SHW == 0 and NH in (1, 2)

    nc = bass.Bass("TRN2", target_bir_lowering=False, debug=False,
                   dynamic_dma_scratch_size=16384)

    t_xT = nc.dram_tensor("xT", [D, SC], F32, kind="ExternalInput")
    t_wt = nc.dram_tensor("wtpr", [C, 128, C, 128], F32, kind="ExternalInput")
    t_mask = nc.dram_tensor("maskn", [D, D], I32, kind="ExternalInput")
    t_pt = nc.dram_tensor("ptpr", [C, 128, C // 2, 128], I32, kind="ExternalInput")
    t_zp2 = nc.dram_tensor("zp2", [128, C], F32, kind="ExternalInput")
    t_sc2 = nc.dram_tensor("sc2", [128, C], F32, kind="ExternalInput")
    t_bi2 = nc.dram_tensor("bi2", [128, C], F32, kind="ExternalInput")
    t_g8a = nc.dram_tensor("g8a", [128, 8], BF16, kind="ExternalInput")
    t_g8b = nc.dram_tensor("g8b", [128, 8], BF16, kind="ExternalInput")
    t_sel = nc.dram_tensor("sel", [32, 512], BF16, kind="ExternalInput")
    t_out = nc.dram_tensor("outT", [D, SC], F32, kind="ExternalOutput")
    t_bmd = nc.dram_tensor("bm_scr", [KBN, KBN], BF16, kind="Internal")
    t_bmx = nc.dram_tensor("bmx_scr", [16 * KBN, KBN], BF16, kind="Internal")
    t_rowd = nc.dram_tensor("row_scr", [1, 2 * SC], F32, kind="Internal")
    t_rsd = nc.dram_tensor("rs_scr", [1, SC], F32, kind="Internal")

    sch = Sched()
    P = sch.op

    with ExitStack() as st_:
        E = st_.enter_context
        arena_a = E(nc.sbuf_tensor("arena_a", [128, C * SC], BF16))
        arena_b = E(nc.sbuf_tensor("arena_b", [128, C * SC], BF16))
        scr_i32 = E(nc.sbuf_tensor("scr_i32", [128, 2 * D], I32))
        mskv = scr_i32.bitcast(BF16)
        scr_bf = E(nc.sbuf_tensor("scr_bf", [128, 4 * D], BF16))
        em = E(nc.sbuf_tensor("em", [128, EMC], BF16))
        crbf = E(nc.sbuf_tensor("crbf", [128, 4 * KBN], BF16))
        bmb = E(nc.sbuf_tensor("bmb", [32, KBT], BF16))
        bmtr = E(nc.sbuf_tensor("bmtr", [32, KBT], BF16))
        onesc = E(nc.sbuf_tensor("onesc", [128, 1], BF16))
        g8a = E(nc.sbuf_tensor("g8a_s", [128, 8], BF16))
        g8b = E(nc.sbuf_tensor("g8b_s", [128, 8], BF16))
        sel_s = E(nc.sbuf_tensor("sel_s", [32, 512], BF16))
        zp2n = E(nc.sbuf_tensor("zp2n", [128, C], BF16))
        sc2s = E(nc.sbuf_tensor("sc2s", [128, C], F32))
        bi2s = E(nc.sbuf_tensor("bi2s", [128, C], BF16))
        mb_b = E(nc.sbuf_tensor("mb_b", [128, SC], BF16))
        rb_b = E(nc.sbuf_tensor("rb_b", [128, SC], F32))
        rsb = E(nc.sbuf_tensor("rsb", [128, SC], BF16))
        sqst = mb_b
        if EMC * 2 == SC * 4:
            stage = em.bitcast(F32)
            stage_b = em
        else:
            stage = E(nc.sbuf_tensor("stage", [128, SC], F32))
            stage_b = stage.bitcast(BF16)
        if 7 * SC + 256 <= 2 * D:
            fout = scr_bf.bitcast(F32)
            rows = fout
            RB = 5 * SC

            onesr_ap = fout[0:1, RB + 2 * SC: RB + 2 * SC + 128]
            epsc_ap = fout[0:1, RB + 2 * SC + 128: RB + 2 * SC + 129]
            LATE_CONST = True
        else:
            fout = E(nc.sbuf_tensor("fout", [128, 2 * SC], F32))
            rowsmall = E(nc.sbuf_tensor("rowsmall", [1, 2 * SC], F32))
            rows = rowsmall
            RB = 0
            onesr_t = E(nc.sbuf_tensor("onesr", [1, 128], F32))
            epsc_t = E(nc.sbuf_tensor("epsc", [1, 1], F32))
            onesr_ap = onesr_t[0:1, :]
            epsc_ap = epsc_t[0:1, :]
            LATE_CONST = False
        ps = [E(nc.psum_tensor("ps%d" % i, [128, SHW], F32)) for i in range(8)]
        sems = {k: E(nc.semaphore(k)) for k in
                ("S_sync", "S_gpsimd", "S_vector", "S_scalar", "S_tensor",
                 "S_taux", "Q_sm", "Q_sg", "Q_b", "Q_r",
                 "Q_x0", "Q_x1", "Q_x2", "Q_x3",
                 "Qm0", "Qm1", "Qm2", "Qm3", "Qsc0", "Qsc1", "Qem0", "Qem1", "Qem2", "Qem3", "Qw0", "Qw1", "Qw2", "Qw3", "Qq0", "Qq1", "Qo0", "Qo1")}
        blk = E(nc.Block())

        def dma(eng, qsem, out_ap, in_ap, reads=(), writes=(), record=None,
                selfwait=False):
            tok = P(eng, lambda o, a=out_ap, b=in_ap: o.dma_start(out=a, in_=b),
                    reads=reads, writes=writes, sem=qsem, inc=16, record=record)
            if selfwait:
                P(eng, lambda o: o.nop(), extra=[(qsem, sch.cnt[qsem])], inc=0)
            return tok

        def bank_data(vw, h):
            return (NH * vw + h) % 3

        FFNB = (0, 1, 2, 7)

        def bank_ffn(w, h):
            return FFNB[(NH * w + h) % 4]

        smtot = ("Q_sm", 4 * 16)
        sgtot = ("Q_sg", 2 * 16)
        dma("gpsimd", "Q_sg", zp2n[:], t_zp2.ap(), writes=[("zp2s",)], record=sgtot)
        dma("sync", "Q_sm", sc2s[:], t_sc2.ap(), writes=[("sc2s",)], record=smtot)
        dma("gpsimd", "Q_sg", bi2s[:], t_bi2.ap(), writes=[("bi2s",)], record=sgtot)
        dma("sync", "Q_sm", g8a[:], t_g8a.ap(), writes=[("g8a",)], record=smtot)
        dma("sync", "Q_sm", g8b[:], t_g8b.ap(), writes=[("g8b",)], record=smtot)
        dma("sync", "Q_sm", sel_s[:], t_sel.ap(), writes=[("sel",)], record=smtot)
        P("vector", lambda o: o.memset(onesc[:], 1.0), writes=[("ones",)])
        if not LATE_CONST:
            P("vector", lambda o: o.memset(onesr_ap, 1.0), writes=[("onesr",)])
            P("vector", lambda o: o.memset(epsc_ap, 1e-5), writes=[("epsc",)])
        P("vector", lambda o: o.memset(bmb[:], 0.0), writes=[("bmb",)])
        P("vector", lambda o: o.memset(bmtr[:], 0.0), writes=[("bmtr",)])
        P("vector", lambda o: o.tensor_scalar(zp2n[:], zp2n[:], -1.0, None, ALU.mult),
          reads=[("zp2s",)], writes=[("zp2n",)])

        NXD = min(4, XG)
        XDIV = C // NXD
        xT_r = t_xT.ap().rearrange("(g i p) s -> g p i s", i=XDIV, p=128)

        def xq_load(g):
            out_ap = arena_a[:, XDIV * SC * g: XDIV * SC * (g + 1)].rearrange(
                "p (i s) -> p i s", s=SC)
            dma("gpsimd", "Q_x%d" % g, out_ap, xT_r[g], writes=[("xq", g)])

        def mask_stage_a(m):
            for half in range(2):
                i = 2 * m + half
                slot = i % 4
                cslot = i % 4
                msl = mskv[:, D * slot: D * (slot + 1)]
                dma("gpsimd", "Qm%d" % slot, msl,
                    t_mask.ap()[128 * i: 128 * (i + 1), :],
                    writes=[("msl", slot)])
                def red_fn(o, d=cslot, sl=slot):
                    mv = bass.AP(mskv, D * sl, [[4 * D, 128], [1, KBN], [KBN, 32]])
                    with nc.allow_low_precision(reason="int 0/1 sums <= 1024 exact in bf16"):
                        return o.tensor_reduce(crbf[:, KBN * d: KBN * (d + 1)],
                                               mv, AX.X, ALU.add)
                P("vector", red_fn, reads=[("msl", slot)], writes=[("crb", cslot)])

        def mask_stage_b(m):
            for half in range(2):
                i = 2 * m + half
                cslot = i % 4
                gg = g8a if half == 0 else g8b
                ggr = ("g8a",) if half == 0 else ("g8b",)
                P("tensor",
                  lambda o, d=cslot, g_=gg, h=half: o.matmul(
                      ps[7][0:8, 0:KBN], g_[:, 0:8],
                      crbf[:, KBN * d: KBN * (d + 1)],
                      start=(h == 0), stop=(h == 1), skip_group_check=True),
                  reads=[("crb", cslot), ggr], writes=[("PB", 7)], sem="S_taux")
            P("vector",
              lambda o: o.tensor_scalar(bmb[0:8, 0:KBN], ps[7][0:8, 0:KBN],
                                        0.0, None, ALU.is_gt),
              reads=[("PB", 7)], writes=[("bmb",)])
            P("vector", lambda o: o.transpose(bmtr[:, 0:KBT], bmb[:, 0:KBT]),
              reads=[("bmb",)], writes=[("bmtr",)])
            rcnt = min(32, KBN)
            jcnt = (KBN + 31) // 32
            qs = "Qsc%d" % (m % 2)
            sct = (qs, sch.cnt.get(qs, 0) + 16 * jcnt)
            for j in range(jcnt):
                in_ap = bass.AP(bmtr, 32 * j, [[KBT, rcnt], [0, 16], [1, 8]])
                out_ap = bass.AP(t_bmx, (16 * 32 * j) * KBN + 8 * m,
                                 [[16 * KBN, rcnt], [KBN, 16], [1, 8]])
                dma("sync", qs, out_ap, in_ap, reads=[("bmtr",)],
                    writes=([("bmx", m)] if j == 0 else []), record=sct)
            in2 = bass.AP(t_bmx, 8 * m, [[KBN, 128], [128 * KBN, C // 2], [1, 8]])
            out2 = bass.AP(em, 8 * m, [[EMC, 128], [KBN, C // 2], [1, 8]])
            dma("sync", "Qem%d" % (m % 4), out2, in2, reads=[("bmx", m)],
                writes=[("EM", m)])

        def wt_refill(v):
            slot = v % 4
            out_ap = scr_bf[:, D * slot: D * (slot + 1)]
            in_ap = t_wt.ap()[v].rearrange("k c n -> k (c n)")
            dma("gpsimd", "Qw%d" % slot, out_ap, in_ap,
                writes=[("wts", slot), ("wts", slot, 0), ("wts", slot, 1)])

        def wt_mult(v):
            slot = v % 4
            off = D * slot
            for par in range(2):
                in1 = bass.AP(em, 8 * (v // 2),
                              [[EMC, 128], [KBN, C // 2], [1, 8], [0, 16]])
                io = bass.AP(scr_bf, off + 128 * par,
                             [[4 * D, 128], [256, C // 2], [16, 8], [1, 16]])
                P("vector" if par == 0 else "gpsimd",
                  lambda o, a=io, b=in1: o.tensor_tensor(a, a, b, ALU.mult),
                  reads=[("wts", slot), ("EM", v // 2)],
                  writes=[("wts", slot, par)])

        wt_refill(0)
        mask_stage_a(0)
        xq_load(0)
        if C > 1:
            wt_refill(1)
        mask_stage_b(0)
        wt_mult(0)
        if NSL > 1:
            mask_stage_a(1)
        if C > 2:
            wt_refill(2)
        for g_ in range(1, NXD):
            xq_load(g_)

        def stats_mms(vv, src, s1_regions, sq_region, first, last):
            for h in range(NH):
                P("tensor",
                  lambda o, v_=vv, h_=h, s_=src: o.matmul(
                      ps[3 + h_][0:1, 0:SHW], onesc[:, 0:1],
                      s_[:, SC * v_ + SHW * h_: SC * v_ + SHW * (h_ + 1)],
                      start=first, stop=last, skip_group_check=True),
                  reads=[s1_regions(vv, h), ("ones",)], writes=[("PB", 3 + h)],
                  sem="S_taux")
            for h in range(NH):
                P("tensor",
                  lambda o, h_=h, v_=vv: o.matmul(
                      ps[5 + h_][0:1, 0:SHW], onesc[:, 0:1],
                      sqst[:, SHW * h_: SHW * (h_ + 1)],
                      start=first, stop=last, skip_group_check=True),
                  reads=[sq_region, ("ones",)], writes=[("PB", 5 + h)],
                  sem="S_taux")

        for v in range(C):
            if v + 1 < C:
                wt_mult(v + 1)
            if v % 2 == 1:
                if (v + 3) // 2 < NSL:
                    mask_stage_b((v + 3) // 2)
            corder = list(range(0, C, 2)) + list(range(1, C, 2))
            for h in range(NH):
                bk = bank_data(v, h)
                fut = sch.plan("S_tensor")
                for ci, c in enumerate(corder):
                    P("tensor",
                      lambda o, v_=v, h_=h, c_=c, b_=bk, st=(ci == 0),
                      sp=(ci == C - 1): o.matmul(
                          ps[b_][:, 0:SHW],
                          scr_bf[:, D * (v_ % 4) + 128 * c_: D * (v_ % 4) + 128 * (c_ + 1)],
                          arena_a[:, SC * c_ + SHW * h_: SC * c_ + SHW * (h_ + 1)],
                          start=st, stop=sp,
                          skip_group_check=True),
                      reads=[("wts", v % 4, c % 2), ("xq", c // XDIV)],
                      writes=([("PB", bk)] if ci == 0 else []),
                      inc=(1 if ci == C - 1 else 0), future=fut)
            if v >= 1:
                stats_mms(v - 1, arena_b, lambda vv, hh: ("h", vv, hh), ("sq", 0),
                          v - 1 == 0, v - 1 == C - 1)
            for h in range(NH):
                bk = bank_data(v, h)
                P("vector",
                  lambda o, v_=v, h_=h, b_=bk: o.tensor_tensor(
                      arena_b[:, SC * v_ + SHW * h_: SC * v_ + SHW * (h_ + 1)],
                      ps[b_][:, 0:SHW],
                      arena_a[:, SC * v_ + SHW * h_: SC * v_ + SHW * (h_ + 1)],
                      ALU.add),
                  reads=[("PB", bk), ("xq", v // XDIV)], writes=[("h", v, h)])
            P("scalar",
              lambda o, v_=v: o.activation(sqst[:, 0:SC],
                                           arena_b[:, SC * v_: SC * (v_ + 1)],
                                           AF.Square),
              reads=[("h", v, 0), ("h", v, NH - 1)], writes=[("sq", 0)])
            if v == 0 and NSL > 2:
                mask_stage_a(2)
            if v == 0 and NSL > 1:
                mask_stage_b(1)
            if v % 2 == 0 and v // 2 + 3 < NSL:
                mask_stage_a(v // 2 + 3)
            if v + 3 < C:
                wt_refill(v + 3)
        stats_mms(C - 1, arena_b, lambda vv, hh: ("h", vv, hh), ("sq", 0),
                  C - 1 == 0, True)

        def ln_finalize(mrb_tag):
            for h in range(NH):
                P("vector",
                  lambda o, h_=h: o.tensor_scalar(
                      rows[0:1, RB + SHW * h_: RB + SHW * (h_ + 1)], ps[3 + h_][0:1, 0:SHW],
                      1.0 / D, None, ALU.mult),
                  reads=[("PB", 3 + h)],
                  writes=[("rows_m", h), (mrb_tag + "_m",)])
            for seg in range(NH):
                pb = FFNB[seg]
                P("tensor",
                  lambda o, sg=seg, p_=pb: o.matmul(
                      ps[p_][:, 0:SHW], onesr_ap,
                      rows[0:1, RB + SHW * sg: RB + SHW * (sg + 1)],
                      start=True, stop=True, skip_group_check=True),
                  reads=[("rows_m", seg), ("onesr",)],
                  writes=[("PB", pb)], sem="S_taux")
                P("scalar",
                  lambda o, sg=seg, p_=pb: o.copy(
                      mb_b[:, SHW * sg: SHW * (sg + 1)], ps[p_][:, 0:SHW]),
                  reads=[("PB", pb)],
                  writes=[(mrb_tag + "_m",), ("rows_m", seg), ("sq", 0)])
            for h in range(NH):
                P("vector",
                  lambda o, h_=h: o.tensor_scalar(
                      rows[0:1, RB + SC + SHW * h_: RB + SC + SHW * (h_ + 1)],
                      ps[5 + h_][0:1, 0:SHW], 1.0 / D, None, ALU.mult),
                  reads=[("PB", 5 + h)],
                  writes=[("rows_s", h), (mrb_tag + "_r",)])
            if 2 * SC <= D:
                msq = fout[0:1, 4 * SC: 5 * SC]
                msq_w = [("pst",)]
                msq_r = [("pst",)]
            else:
                msq = stage[0:1, 0:SC]
                msq_w = [("pst",), ("stage", 0), ("stage", NH - 1)]
                msq_r = list(msq_w)
            P("vector",
              lambda o: o.tensor_tensor(msq, rows[0:1, RB: RB + SC],
                                        rows[0:1, RB: RB + SC], ALU.mult),
              reads=[("rows_m", 0), ("rows_m", NH - 1)], writes=msq_w)
            P("vector",
              lambda o: o.tensor_tensor(rows[0:1, RB + SC: RB + 2 * SC],
                                        rows[0:1, RB + SC: RB + 2 * SC], msq,
                                        ALU.subtract),
              reads=[("rows_s", 0), ("rows_s", NH - 1)] + msq_r,
              writes=[("rows_s", 0), ("rows_s", NH - 1)])
            P("scalar",
              lambda o: o.activation(rows[0:1, RB + SC: RB + 2 * SC],
                                     rows[0:1, RB + SC: RB + 2 * SC],
                                     AF.Ln, bias=epsc_ap),
              reads=[("rows_s", 0), ("epsc",)],
              writes=[("rows_s", 0), ("rows_s", NH - 1)])
            P("scalar",
              lambda o: o.activation(rows[0:1, RB + SC: RB + 2 * SC],
                                     rows[0:1, RB + SC: RB + 2 * SC],
                                     AF.Exp, scale=-0.5),
              reads=[("rows_s", 0)], writes=[("rows_s", 0), ("rows_s", NH - 1)])
            for seg in range(NH):
                pb = FFNB[2 + seg]
                P("tensor",
                  lambda o, sg=NH + seg, p_=pb: o.matmul(
                      ps[p_][:, 0:SHW], onesr_ap,
                      rows[0:1, RB + SHW * sg: RB + SHW * (sg + 1)],
                      start=True, stop=True, skip_group_check=True),
                  reads=[("rows_s", 0), ("rows_s", NH - 1), ("onesr",)],
                  writes=[("PB", pb)], sem="S_taux")
                P("scalar",
                  lambda o, sg=seg, p_=pb: o.copy(
                      rb_b[:, SHW * sg: SHW * (sg + 1)], ps[p_][:, 0:SHW]),
                  reads=[("PB", pb)],
                  writes=[(mrb_tag + "_r",), ("rows_s", seg)])

        if LATE_CONST:
            P("vector", lambda o: o.memset(onesr_ap, 1.0),
              writes=[("onesr",), ("wts", 3), ("wts", 3, 0), ("wts", 3, 1)])
            P("vector", lambda o: o.memset(epsc_ap, 1e-5), writes=[("epsc",)])
        ln_finalize("mrb")

        def q_dma(w):
            slot = w % 2
            q0 = (D // 2) * slot
            out_ap = scr_i32[:, q0: q0 + D // 2]
            dma("scalar", "Qq%d" % slot, out_ap,
                t_pt.ap()[w].rearrange("j jt b -> j (jt b)"),
                writes=[("qsl", slot), ("msl", slot)])

        def unpack(w):
            slot = w % 2
            q0 = (D // 2) * slot
            qs = scr_i32[:, q0: q0 + D // 2]
            P("vector",
              lambda o, q=qs: o.tensor_scalar(scr_i32[:, D: D + D // 2], q, 15, None,
                                              ALU.bitwise_and),
              reads=[("qsl", slot)], writes=[("l32",), ("msl", 2)])
            P("vector",
              lambda o, q=qs: o.tensor_scalar(scr_i32[:, D + D // 2: 2 * D], q, 4, 15,
                                              ALU.logical_shift_right, ALU.bitwise_and),
              reads=[("qsl", slot)], writes=[("h32",), ("msl", 3)])
            P("scalar",
              lambda o, s=slot: o.copy(scr_bf[:, D * s: D * s + D // 2],
                                       scr_i32[:, D: D + D // 2]),
              reads=[("l32",)], writes=[("lobf", slot), ("wts", slot)])
            P("scalar",
              lambda o, s=slot: o.copy(scr_bf[:, D * s + D // 2: D * (s + 1)],
                                       scr_i32[:, D + D // 2: 2 * D]),
              reads=[("h32",)], writes=[("hibf", slot), ("wts", slot)])

        def ffn_mm(w, h, c, first, last, fut):
            slot = w % 2
            jt = c // 2
            par = c % 2
            woff = D * slot + (D // 2) * par + 128 * jt
            bk = bank_ffn(w, h)
            return P("tensor",
                     lambda o, w_=w, h_=h, c_=c, b_=bk, wo=woff: o.matmul(
                         ps[b_][:, 0:SHW], scr_bf[:, wo: wo + 128],
                         arena_b[:, SC * c_ + SHW * h_: SC * c_ + SHW * (h_ + 1)],
                         start=first, stop=last, skip_group_check=True),
                     reads=[("lobf", slot) if par == 0 else ("hibf", slot),
                            ("hT", c)],
                     writes=([("PB", bk)] if first else []),
                     inc=(1 if last else 0), future=fut)

        def epilogue(w):
            for h in range(NH):
                bk = bank_ffn(w, h)
                P("vector",
                  lambda o, w_=w, h_=h, b_=bk: o.scalar_tensor_tensor(
                      stage[:, SHW * h_: SHW * (h_ + 1)],
                      rsb[:, SHW * h_: SHW * (h_ + 1)], zp2n[:, w_: w_ + 1],
                      ps[b_][:, 0:SHW], ALU.mult, ALU.add),
                  reads=[("PB", bk), ("rsb",), ("zp2n",)], writes=[("stage", h)])
                P("scalar",
                  lambda o, w_=w, h_=h: o.activation(
                      stage[:, SHW * h_: SHW * (h_ + 1)],
                      stage[:, SHW * h_: SHW * (h_ + 1)], AF.Identity,
                      bias=bi2s[:, w_: w_ + 1], scale=sc2s[:, w_: w_ + 1]),
                  reads=[("stage", h), ("bi2s",), ("sc2s",)], writes=[("stage", h)])
                P("vector",
                  lambda o, w_=w, h_=h: o.tensor_tensor(
                      arena_a[:, SC * w_ + SHW * h_: SC * w_ + SHW * (h_ + 1)],
                      stage[:, SHW * h_: SHW * (h_ + 1)],
                      arena_b[:, SC * w_ + SHW * h_: SC * w_ + SHW * (h_ + 1)],
                      ALU.add),
                  reads=[("stage", h), ("hT", w)],
                  writes=[("x2", w, h), ("xq", w // XDIV)])
            P("scalar",
              lambda o, w_=w: o.activation(sqst[:, 0:SC],
                                           arena_a[:, SC * w_: SC * (w_ + 1)],
                                           AF.Square),
              reads=[("x2", w, 0), ("x2", w, NH - 1)],
              writes=[("sq", 0), ("mrb_m",)])

        q_dma(0)
        if C > 1:
            q_dma(1)
        unpack(0)
        futs0 = [sch.plan("S_tensor", ahead=1 + h) for h in range(NH)]
        for c in range(C):
            eng = "gpsimd" if (c % 3 == 2 and C > 4) else "vector"
            if eng == "vector":
                stg = stage[:, 0:SC]
                stgr = [("stage", 0), ("stage", NH - 1)]
            else:
                stg = fout[:, 4 * SC: 5 * SC]
                stgr = [("pst",)]
            P(eng,
              lambda o, c_=c, sg=stg: o.tensor_tensor(
                  sg, arena_b[:, SC * c_: SC * (c_ + 1)],
                  mb_b[:], ALU.subtract),
              reads=[("h", c, 0), ("h", c, NH - 1), ("mrb_m",)],
              writes=stgr +
                     ([("EM", m) for m in range(NSL)] if c == 0 else []) +
                     ([("wts", 2), ("wts", 2, 0), ("wts", 2, 1)]
                      if eng == "gpsimd" and c == 3 else []))
            P(eng,
              lambda o, c_=c, sg=stg: o.tensor_tensor(
                  arena_b[:, SC * c_: SC * (c_ + 1)],
                  sg, rb_b[:], ALU.mult),
              reads=stgr + [("mrb_r",)],
              writes=[("hT", c), ("h", c, 0), ("h", c, NH - 1)])
            for h in range(NH):
                ffn_mm(0, h, c, c == 0, c == C - 1, futs0[h])
            for h in range(NH):
                P("tensor",
                  lambda o, c_=c, h_=h: o.matmul(
                      ps[3 + h_][0:1, 0:SHW], onesc[:, 0:1],
                      arena_b[:, SC * c_ + SHW * h_: SC * c_ + SHW * (h_ + 1)],
                      start=(c_ == 0), stop=(c_ == C - 1), skip_group_check=True),
                  reads=[("hT", c), ("ones",)], writes=[("PB", 3 + h)],
                  sem="S_taux")
        for h in range(NH):
            P("vector",
              lambda o, h_=h: o.tensor_copy(rows[0:1, RB + SHW * h_: RB + SHW * (h_ + 1)],
                                            ps[3 + h_][0:1, 0:SHW]),
              reads=[("PB", 3 + h)], writes=[("rows_m", h), ("mrb",)])
        for seg in range(NH):
            pb = (2, 7)[seg % 2]
            P("tensor",
              lambda o, sg=seg, p_=pb: o.matmul(
                  ps[p_][:, 0:SHW], onesr_ap,
                  rows[0:1, RB + SHW * sg: RB + SHW * (sg + 1)],
                  start=True, stop=True, skip_group_check=True),
              reads=[("rows_m", 0), ("rows_m", NH - 1), ("onesr",)],
              writes=[("PB", pb)], sem="S_taux")
            P("scalar",
              lambda o, sg=seg, p_=pb: o.copy(
                  rsb[:, SHW * sg: SHW * (sg + 1)], ps[p_][:, 0:SHW]),
              reads=[("PB", pb)], writes=[("rsb",)])
        unpack(1)
        epilogue(0)

        for w in range(1, C):
            if w + 1 < C:
                q_dma(w + 1)
            if w + 1 < C:
                unpack(w + 1)
            corder = list(range(0, C, 2)) + list(range(1, C, 2))
            for h in range(NH):
                fut = sch.plan("S_tensor")
                for ci, c in enumerate(corder):
                    ffn_mm(w, h, c, ci == 0, ci == C - 1, fut)
            stats_mms(w - 1, arena_a, lambda vv, hh: ("x2", vv, hh), ("sq", 0),
                      w - 1 == 0, w - 1 == C - 1)
            epilogue(w)
        stats_mms(C - 1, arena_a, lambda vv, hh: ("x2", vv, hh), ("sq", 0),
                  C - 1 == 0, True)

        ln_finalize("mrb2")
        for c in range(C):
            eng = "gpsimd" if ((3 * c) % 7 < 3 and C > 4) else "vector"
            if eng == "vector":
                stg = stage[:, 0:SC]
                stgr = [("stage", 0), ("stage", NH - 1)]
            else:
                stg = fout[:, 4 * SC: 5 * SC]
                stgr = [("pst",)]
            P(eng,
              lambda o, c_=c, sg=stg: o.tensor_tensor(
                  sg, arena_a[:, SC * c_: SC * (c_ + 1)],
                  mb_b[:], ALU.subtract),
              reads=[("x2", c, 0), ("x2", c, NH - 1), ("mrb2_m",)],
              writes=stgr)
            P(eng,
              lambda o, c_=c, sg=stg: o.tensor_tensor(
                  fout[:, SC * (c_ % 2): SC * (c_ % 2 + 1)],
                  sg, rb_b[:], ALU.mult),
              reads=stgr + [("mrb2_r",)],
              writes=[("fo", c % 2), ("lobf", 0) if c % 2 == 0 else ("hibf", 0)])
            dma("scalar", "Qo%d" % (c % 2), t_out.ap()[128 * c: 128 * (c + 1), :],
                fout[:, SC * (c % 2): SC * (c % 2 + 1)],
                reads=[("fo", c % 2)], writes=[("outd", c)])
        P("sync", lambda o: o.nop(),
          extra=[("Qo0", sch.cnt.get("Qo0", 0)), ("Qo1", sch.cnt.get("Qo1", 0))],
          inc=0)

        @blk.sync
        def _(e):
            sch.emit_engine("sync", e, sems)

        @blk.gpsimd
        def _(e):
            sch.emit_engine("gpsimd", e, sems)

        @blk.vector
        def _(e):
            sch.emit_engine("vector", e, sems)

        @blk.scalar
        def _(e):
            sch.emit_engine("scalar", e, sems)

        @blk.tensor
        def _(e):
            sch.emit_engine("tensor", e, sems)

    nc._t_out = t_out
    return nc


def host_prepare(D, SC_total, x, W, mask, packed, scale, zp, bias, n_cores):
    pi = _perm(D)
    C = D // 128
    T = x.shape[0]
    SC = T // n_cores
    xT = np.ascontiguousarray(x.T)[pi]
    wtp = W[np.ix_(pi, pi)].T
    wtpr = np.ascontiguousarray(
        wtp.reshape(C, 128, C, 128).transpose(2, 1, 0, 3))
    P = packed.reshape(D, D // 2)
    ptp = P[pi].T
    ptpr = np.ascontiguousarray(
        ptp.reshape(C // 2, 128, C, 128).transpose(2, 1, 0, 3))
    zp2 = np.ascontiguousarray(zp.reshape(D)[pi].reshape(C, 128).T)
    sc2 = np.ascontiguousarray(scale.reshape(D)[pi].reshape(C, 128).T)
    bi2 = np.ascontiguousarray(bias.reshape(D)[pi].reshape(C, 128).T)
    ar = np.arange(128)
    g8a = (ar[:, None] // 32 == np.arange(8)[None, :]).astype(BFNP)
    g8b = (ar[:, None] // 32 + 4 == np.arange(8)[None, :]).astype(BFNP)
    r32 = np.arange(32)
    sel = np.concatenate(
        [(r32[:, None] == 8 * q + ar[None, :] // 16) for q in range(4)],
        axis=1).astype(BFNP)
    kbn = D // 32
    maskn = np.ascontiguousarray(
        mask.astype(np.int32, copy=False).reshape(D, kbn, 32)
        .transpose(0, 2, 1).reshape(D, D))
    shared = dict(wtpr=wtpr.astype(np.float32, copy=False), maskn=maskn, sel=sel,
                  ptpr=ptpr.astype(np.int32, copy=False),
                  zp2=zp2.astype(np.float32, copy=False),
                  sc2=sc2.astype(np.float32, copy=False),
                  bi2=bi2.astype(np.float32, copy=False), g8a=g8a, g8b=g8b)
    in_maps = []
    for cix in range(n_cores):
        m = dict(shared)
        m["xT"] = np.ascontiguousarray(xT[:, SC * cix: SC * (cix + 1)])
        in_maps.append(m)
    inv = np.empty(D, np.int64)
    inv[pi] = np.arange(D)
    return in_maps, inv, SC


def kernel(L0_attn_data, L0_attn_weight, L0_attn_mask, L0_ffn_weight_packed,
           L0_ffn_scale, L0_ffn_zero_point, L0_ffn_bias):
    D = L0_attn_weight.shape[0]
    B, S, _ = L0_attn_data.shape
    x = np.ascontiguousarray(np.asarray(L0_attn_data, dtype=np.float32).reshape(B * S, D))
    in_maps, inv, SC = host_prepare(
        D, B * S, x, np.asarray(L0_attn_weight, np.float32),
        np.asarray(L0_attn_mask), np.asarray(L0_ffn_weight_packed, np.int32),
        np.asarray(L0_ffn_scale, np.float32), np.asarray(L0_ffn_zero_point, np.float32),
        np.asarray(L0_ffn_bias, np.float32), N_CORES)
    nc = build_program(D, SC)
    res = run_bass_kernel_spmd(nc, in_maps, core_ids=list(range(N_CORES)))
    out = np.empty((B * S, D), np.float32)
    for cix in range(N_CORES):
        outT = np.asarray(res.results[cix]["outT"])
        out[SC * cix: SC * (cix + 1)] = outT.T[:, inv]
    return out.reshape(B, S, D)


# version 44
# speedup vs baseline: 1.1522x; 1.0249x over previous
import numpy as np
import ml_dtypes
from contextlib import ExitStack

import concourse.bass as bass
import concourse.mybir as mybir
from concourse.bass_utils import run_bass_kernel_spmd

F32 = mybir.dt.float32
BF16 = mybir.dt.bfloat16
I32 = mybir.dt.int32
AF = mybir.ActivationFunctionType
ALU = mybir.AluOpType
AX = mybir.AxisListType
BFNP = ml_dtypes.bfloat16

N_CORES = 8


def _perm(d):
    pi = np.empty(d, np.int64)
    ar = np.arange(128)
    for t in range(d // 256):
        b = 256 * t
        pi[b:b + 128] = b + 2 * ar
        pi[b + 128:b + 256] = b + 1 + 2 * ar
    return pi


class Sched:
    ENGINES = ("sync", "gpsimd", "vector", "scalar", "tensor")

    def __init__(self):
        self.ev = {e: [] for e in self.ENGINES}
        self.cnt = {}
        self.state = {}

    def plan(self, sem, ahead=1):
        return (sem, self.cnt.get(sem, 0) + ahead)

    def op(self, eng, fn, reads=(), writes=(), sem=None, inc=1, extra=(),
           future=None, record=None):
        sem = sem or ("S_" + eng)
        waits = {}

        def need(tok):
            if tok is None:
                return
            s, v = tok
            if waits.get(s, 0) < v:
                waits[s] = v

        for r in reads:
            st = self.state.get(r)
            if st:
                need(st[0])
        for wr in writes:
            st = self.state.get(wr)
            if st:
                need(st[0])
                for s, v in st[1].items():
                    need((s, v))
        for tok in extra:
            need(tok)
        if inc:
            newv = self.cnt.get(sem, 0) + inc
            self.cnt[sem] = newv
            tok = (sem, newv)
        else:
            tok = future
        if record is not None:
            tok = record
        self.ev[eng].append((waits, fn, (sem, inc) if inc else None))
        for r in reads:
            st = self.state.setdefault(r, [None, {}])
            if tok is not None:
                s, v = tok
                st[1][s] = max(st[1].get(s, 0), v)
        for wr in writes:
            self.state[wr] = [tok, {}]
        return tok

    def emit_engine(self, e, obj, sems):
        hw = {}
        for waits, fn, inc in self.ev[e]:
            for s in sorted(waits):
                v = waits[s]
                if hw.get(s, 0) >= v:
                    continue
                obj.wait_ge(sems[s], v)
                hw[s] = v
            ins = fn(obj)
            if inc is not None:
                s, n = inc
                ins.then_inc(sems[s], n)


def build_program(D, SC):
    C = D // 128
    NSL = D // 256
    XG = C // 4
    SHW = min(512, SC)
    NH = SC // SHW
    KBN = D // 32
    KBT = max(32, KBN)
    EMC = (C // 2) * KBN
    assert SC % SHW == 0 and NH in (1, 2)

    nc = bass.Bass("TRN2", target_bir_lowering=False, debug=False,
                   dynamic_dma_scratch_size=16384)

    t_xT = nc.dram_tensor("xT", [D, SC], F32, kind="ExternalInput")
    t_wt = nc.dram_tensor("wtpr", [C, 128, C, 128], F32, kind="ExternalInput")
    t_mask = nc.dram_tensor("maskn", [D, D], I32, kind="ExternalInput")
    t_pt = nc.dram_tensor("ptpr", [C, 128, C // 2, 128], I32, kind="ExternalInput")
    t_zp2 = nc.dram_tensor("zp2", [128, C], F32, kind="ExternalInput")
    t_sc2 = nc.dram_tensor("sc2", [128, C], F32, kind="ExternalInput")
    t_bi2 = nc.dram_tensor("bi2", [128, C], F32, kind="ExternalInput")
    t_g8a = nc.dram_tensor("g8a", [128, 8], BF16, kind="ExternalInput")
    t_g8b = nc.dram_tensor("g8b", [128, 8], BF16, kind="ExternalInput")
    t_sel = nc.dram_tensor("sel", [32, 512], BF16, kind="ExternalInput")
    t_out = nc.dram_tensor("outT", [D, SC], F32, kind="ExternalOutput")
    t_bmd = nc.dram_tensor("bm_scr", [KBN, KBN], BF16, kind="Internal")
    t_bmx = nc.dram_tensor("bmx_scr", [16 * KBN, KBN], BF16, kind="Internal")
    t_rowd = nc.dram_tensor("row_scr", [1, 2 * SC], F32, kind="Internal")
    t_rsd = nc.dram_tensor("rs_scr", [1, SC], F32, kind="Internal")

    sch = Sched()
    P = sch.op

    with ExitStack() as st_:
        E = st_.enter_context
        arena_a = E(nc.sbuf_tensor("arena_a", [128, C * SC], BF16))
        arena_b = E(nc.sbuf_tensor("arena_b", [128, C * SC], BF16))
        scr_i32 = E(nc.sbuf_tensor("scr_i32", [128, 2 * D], I32))
        mskv = scr_i32.bitcast(BF16)
        scr_bf = E(nc.sbuf_tensor("scr_bf", [128, 4 * D], BF16))
        em = E(nc.sbuf_tensor("em", [128, EMC], BF16))
        crbf = E(nc.sbuf_tensor("crbf", [128, 4 * KBN], BF16))
        bmb = E(nc.sbuf_tensor("bmb", [32, KBT], BF16))
        bmtr = E(nc.sbuf_tensor("bmtr", [32, KBT], BF16))
        onesc = E(nc.sbuf_tensor("onesc", [128, 1], BF16))
        g8a = E(nc.sbuf_tensor("g8a_s", [128, 8], BF16))
        g8b = E(nc.sbuf_tensor("g8b_s", [128, 8], BF16))
        sel_s = E(nc.sbuf_tensor("sel_s", [32, 512], BF16))
        zp2n = E(nc.sbuf_tensor("zp2n", [128, C], BF16))
        sc2s = E(nc.sbuf_tensor("sc2s", [128, C], F32))
        bi2s = E(nc.sbuf_tensor("bi2s", [128, C], BF16))
        mb_b = E(nc.sbuf_tensor("mb_b", [128, SC], BF16))
        rb_b = E(nc.sbuf_tensor("rb_b", [128, SC], F32))
        rsb = E(nc.sbuf_tensor("rsb", [128, SC], BF16))
        sqst = mb_b
        if EMC * 2 == SC * 4:
            stage = em.bitcast(F32)
            stage_b = em
        else:
            stage = E(nc.sbuf_tensor("stage", [128, SC], F32))
            stage_b = stage.bitcast(BF16)
        if 7 * SC + 256 <= 2 * D:
            fout = scr_bf.bitcast(F32)
            rows = fout
            RB = 5 * SC

            onesr_ap = fout[0:1, RB + 2 * SC: RB + 2 * SC + 128]
            epsc_ap = fout[0:1, RB + 2 * SC + 128: RB + 2 * SC + 129]
            LATE_CONST = True
        else:
            fout = E(nc.sbuf_tensor("fout", [128, 2 * SC], F32))
            rowsmall = E(nc.sbuf_tensor("rowsmall", [1, 2 * SC], F32))
            rows = rowsmall
            RB = 0
            onesr_t = E(nc.sbuf_tensor("onesr", [1, 128], F32))
            epsc_t = E(nc.sbuf_tensor("epsc", [1, 1], F32))
            onesr_ap = onesr_t[0:1, :]
            epsc_ap = epsc_t[0:1, :]
            LATE_CONST = False
        ps = [E(nc.psum_tensor("ps%d" % i, [128, SHW], F32)) for i in range(8)]
        sems = {k: E(nc.semaphore(k)) for k in
                ("S_sync", "S_gpsimd", "S_vector", "S_scalar", "S_tensor",
                 "S_taux", "Q_sm", "Q_sg", "Q_b", "Q_r",
                 "Q_x0", "Q_x1", "Q_x2", "Q_x3",
                 "Qm0", "Qm1", "Qm2", "Qm3", "Qsc0", "Qsc1", "Qem0", "Qem1", "Qem2", "Qem3", "Qw0", "Qw1", "Qw2", "Qw3", "Qq0", "Qq1", "Qo0", "Qo1")}
        blk = E(nc.Block())

        def dma(eng, qsem, out_ap, in_ap, reads=(), writes=(), record=None,
                selfwait=False):
            tok = P(eng, lambda o, a=out_ap, b=in_ap: o.dma_start(out=a, in_=b),
                    reads=reads, writes=writes, sem=qsem, inc=16, record=record)
            if selfwait:
                P(eng, lambda o: o.nop(), extra=[(qsem, sch.cnt[qsem])], inc=0)
            return tok

        def bank_data(vw, h):
            return (NH * vw + h) % 3

        FFNB = (0, 1, 2, 7)

        def bank_ffn(w, h):
            return FFNB[(NH * w + h) % 4]

        smtot = ("Q_sm", 4 * 16)
        sgtot = ("Q_sg", 2 * 16)
        dma("gpsimd", "Q_sg", zp2n[:], t_zp2.ap(), writes=[("zp2s",)], record=sgtot)
        dma("sync", "Q_sm", sc2s[:], t_sc2.ap(), writes=[("sc2s",)], record=smtot)
        dma("gpsimd", "Q_sg", bi2s[:], t_bi2.ap(), writes=[("bi2s",)], record=sgtot)
        dma("sync", "Q_sm", g8a[:], t_g8a.ap(), writes=[("g8a",)], record=smtot)
        dma("sync", "Q_sm", g8b[:], t_g8b.ap(), writes=[("g8b",)], record=smtot)
        dma("sync", "Q_sm", sel_s[:], t_sel.ap(), writes=[("sel",)], record=smtot)
        P("vector", lambda o: o.memset(onesc[:], 1.0), writes=[("ones",)])
        if not LATE_CONST:
            P("vector", lambda o: o.memset(onesr_ap, 1.0), writes=[("onesr",)])
            P("vector", lambda o: o.memset(epsc_ap, 1e-5), writes=[("epsc",)])
        P("vector", lambda o: o.memset(bmb[:], 0.0), writes=[("bmb",)])
        P("vector", lambda o: o.memset(bmtr[:], 0.0), writes=[("bmtr",)])
        P("vector", lambda o: o.tensor_scalar(zp2n[:], zp2n[:], -1.0, None, ALU.mult),
          reads=[("zp2s",)], writes=[("zp2n",)])

        NXD = min(4, XG)
        XDIV = C // NXD
        xT_r = t_xT.ap().rearrange("(g i p) s -> g p i s", i=XDIV, p=128)

        def xq_load(g):
            out_ap = arena_a[:, XDIV * SC * g: XDIV * SC * (g + 1)].rearrange(
                "p (i s) -> p i s", s=SC)
            dma("gpsimd", "Q_x%d" % g, out_ap, xT_r[g], writes=[("xq", g)])

        def mask_stage_a(m, dmas_only=False, reds_only=False):
            for half in range(2):
                i = 2 * m + half
                slot = i % 4
                cslot = i % 4
                if not reds_only:
                    msl = mskv[:, D * slot: D * (slot + 1)]
                    dma("gpsimd", "Qm%d" % slot, msl,
                        t_mask.ap()[128 * i: 128 * (i + 1), :],
                        writes=[("msl", slot)])
                if dmas_only:
                    continue
                def red_fn(o, d=cslot, sl=slot):
                    mv = bass.AP(mskv, D * sl, [[4 * D, 128], [1, KBN], [KBN, 32]])
                    with nc.allow_low_precision(reason="int 0/1 sums <= 1024 exact in bf16"):
                        return o.tensor_reduce(crbf[:, KBN * d: KBN * (d + 1)],
                                               mv, AX.X, ALU.add)
                P("vector", red_fn, reads=[("msl", slot)], writes=[("crb", cslot)])

        def mask_stage_b(m):
            for half in range(2):
                i = 2 * m + half
                cslot = i % 4
                gg = g8a if half == 0 else g8b
                ggr = ("g8a",) if half == 0 else ("g8b",)
                P("tensor",
                  lambda o, d=cslot, g_=gg, h=half: o.matmul(
                      ps[7][0:8, 0:KBN], g_[:, 0:8],
                      crbf[:, KBN * d: KBN * (d + 1)],
                      start=(h == 0), stop=(h == 1), skip_group_check=True),
                  reads=[("crb", cslot), ggr], writes=[("PB", 7)], sem="S_taux")
            P("vector",
              lambda o: o.tensor_scalar(bmb[0:8, 0:KBN], ps[7][0:8, 0:KBN],
                                        0.0, None, ALU.is_gt),
              reads=[("PB", 7)], writes=[("bmb",)])
            P("vector", lambda o: o.transpose(bmtr[:, 0:KBT], bmb[:, 0:KBT]),
              reads=[("bmb",)], writes=[("bmtr",)])
            rcnt = min(32, KBN)
            jcnt = (KBN + 31) // 32
            qs = "Qsc%d" % (m % 2)
            sct = (qs, sch.cnt.get(qs, 0) + 16 * jcnt)
            for j in range(jcnt):
                in_ap = bass.AP(bmtr, 32 * j, [[KBT, rcnt], [0, 16], [1, 8]])
                out_ap = bass.AP(t_bmx, (16 * 32 * j) * KBN + 8 * m,
                                 [[16 * KBN, rcnt], [KBN, 16], [1, 8]])
                dma("sync", qs, out_ap, in_ap, reads=[("bmtr",)],
                    writes=([("bmx", m)] if j == 0 else []), record=sct)
            in2 = bass.AP(t_bmx, 8 * m, [[KBN, 128], [128 * KBN, C // 2], [1, 8]])
            out2 = bass.AP(em, 8 * m, [[EMC, 128], [KBN, C // 2], [1, 8]])
            dma("sync", "Qem%d" % (m % 4), out2, in2, reads=[("bmx", m)],
                writes=[("EM", m)])

        def wt_refill(v):
            slot = v % 4
            out_ap = scr_bf[:, D * slot: D * (slot + 1)]
            in_ap = t_wt.ap()[v].rearrange("k c n -> k (c n)")
            dma("gpsimd", "Qw%d" % slot, out_ap, in_ap,
                writes=[("wts", slot), ("wts", slot, 0), ("wts", slot, 1)])

        def wt_mult(v):
            slot = v % 4
            off = D * slot
            for par in range(2):
                in1 = bass.AP(em, 8 * (v // 2),
                              [[EMC, 128], [KBN, C // 2], [1, 8], [0, 16]])
                io = bass.AP(scr_bf, off + 128 * par,
                             [[4 * D, 128], [256, C // 2], [16, 8], [1, 16]])
                P("vector" if par == 0 else "gpsimd",
                  lambda o, a=io, b=in1: o.tensor_tensor(a, a, b, ALU.mult),
                  reads=[("wts", slot), ("EM", v // 2)],
                  writes=[("wts", slot, par)])

        wt_refill(0)
        mask_stage_a(0)
        xq_load(0)
        if C > 1:
            wt_refill(1)
        if NSL > 1:
            mask_stage_a(1, dmas_only=True)
        if C > 2:
            wt_refill(2)
        for g_ in range(1, NXD):
            xq_load(g_)
        mask_stage_b(0)
        wt_mult(0)
        if NSL > 1:
            mask_stage_a(1, reds_only=True)

        def stats_mms(vv, src, s1_regions, sq_region, first, last):
            for h in range(NH):
                P("tensor",
                  lambda o, v_=vv, h_=h, s_=src: o.matmul(
                      ps[3 + h_][0:1, 0:SHW], onesc[:, 0:1],
                      s_[:, SC * v_ + SHW * h_: SC * v_ + SHW * (h_ + 1)],
                      start=first, stop=last, skip_group_check=True),
                  reads=[s1_regions(vv, h), ("ones",)], writes=[("PB", 3 + h)],
                  sem="S_taux")
            for h in range(NH):
                P("tensor",
                  lambda o, h_=h, v_=vv: o.matmul(
                      ps[5 + h_][0:1, 0:SHW], onesc[:, 0:1],
                      sqst[:, SHW * h_: SHW * (h_ + 1)],
                      start=first, stop=last, skip_group_check=True),
                  reads=[sq_region, ("ones",)], writes=[("PB", 5 + h)],
                  sem="S_taux")

        for v in range(C):
            if v + 1 < C:
                wt_mult(v + 1)
            if v % 2 == 1:
                if (v + 3) // 2 < NSL:
                    mask_stage_b((v + 3) // 2)
            corder = list(range(0, C, 2)) + list(range(1, C, 2))
            for h in range(NH):
                bk = bank_data(v, h)
                fut = sch.plan("S_tensor")
                for ci, c in enumerate(corder):
                    P("tensor",
                      lambda o, v_=v, h_=h, c_=c, b_=bk, st=(ci == 0),
                      sp=(ci == C - 1): o.matmul(
                          ps[b_][:, 0:SHW],
                          scr_bf[:, D * (v_ % 4) + 128 * c_: D * (v_ % 4) + 128 * (c_ + 1)],
                          arena_a[:, SC * c_ + SHW * h_: SC * c_ + SHW * (h_ + 1)],
                          start=st, stop=sp,
                          skip_group_check=True),
                      reads=[("wts", v % 4, c % 2), ("xq", c // XDIV)],
                      writes=([("PB", bk)] if ci == 0 else []),
                      inc=(1 if ci == C - 1 else 0), future=fut)
            if v >= 1:
                stats_mms(v - 1, arena_b, lambda vv, hh: ("h", vv, hh), ("sq", 0),
                          v - 1 == 0, v - 1 == C - 1)
            for h in range(NH):
                bk = bank_data(v, h)
                P("vector",
                  lambda o, v_=v, h_=h, b_=bk: o.tensor_tensor(
                      arena_b[:, SC * v_ + SHW * h_: SC * v_ + SHW * (h_ + 1)],
                      ps[b_][:, 0:SHW],
                      arena_a[:, SC * v_ + SHW * h_: SC * v_ + SHW * (h_ + 1)],
                      ALU.add),
                  reads=[("PB", bk), ("xq", v // XDIV)], writes=[("h", v, h)])
            P("scalar",
              lambda o, v_=v: o.activation(sqst[:, 0:SC],
                                           arena_b[:, SC * v_: SC * (v_ + 1)],
                                           AF.Square),
              reads=[("h", v, 0), ("h", v, NH - 1)], writes=[("sq", 0)])
            if v == 0 and NSL > 2:
                mask_stage_a(2)
            if v == 0 and NSL > 1:
                mask_stage_b(1)
            if v % 2 == 0 and v // 2 + 3 < NSL:
                mask_stage_a(v // 2 + 3)
            if v + 3 < C:
                wt_refill(v + 3)
        stats_mms(C - 1, arena_b, lambda vv, hh: ("h", vv, hh), ("sq", 0),
                  C - 1 == 0, True)

        def ln_finalize(mrb_tag):
            for h in range(NH):
                P("vector",
                  lambda o, h_=h: o.tensor_scalar(
                      rows[0:1, RB + SHW * h_: RB + SHW * (h_ + 1)], ps[3 + h_][0:1, 0:SHW],
                      1.0 / D, None, ALU.mult),
                  reads=[("PB", 3 + h)],
                  writes=[("rows_m", h), (mrb_tag + "_m",)])
            for seg in range(NH):
                pb = FFNB[seg]
                P("tensor",
                  lambda o, sg=seg, p_=pb: o.matmul(
                      ps[p_][:, 0:SHW], onesr_ap,
                      rows[0:1, RB + SHW * sg: RB + SHW * (sg + 1)],
                      start=True, stop=True, skip_group_check=True),
                  reads=[("rows_m", seg), ("onesr",)],
                  writes=[("PB", pb)], sem="S_taux")
                P("scalar",
                  lambda o, sg=seg, p_=pb: o.copy(
                      mb_b[:, SHW * sg: SHW * (sg + 1)], ps[p_][:, 0:SHW]),
                  reads=[("PB", pb)],
                  writes=[(mrb_tag + "_m",), ("rows_m", seg), ("sq", 0)])
            for h in range(NH):
                P("vector",
                  lambda o, h_=h: o.tensor_scalar(
                      rows[0:1, RB + SC + SHW * h_: RB + SC + SHW * (h_ + 1)],
                      ps[5 + h_][0:1, 0:SHW], 1.0 / D, None, ALU.mult),
                  reads=[("PB", 5 + h)],
                  writes=[("rows_s", h), (mrb_tag + "_r",)])
            if 2 * SC <= D:
                msq = fout[0:1, 4 * SC: 5 * SC]
                msq_w = [("pst",)]
                msq_r = [("pst",)]
            else:
                msq = stage[0:1, 0:SC]
                msq_w = [("pst",), ("stage", 0), ("stage", NH - 1)]
                msq_r = list(msq_w)
            P("vector",
              lambda o: o.tensor_tensor(msq, rows[0:1, RB: RB + SC],
                                        rows[0:1, RB: RB + SC], ALU.mult),
              reads=[("rows_m", 0), ("rows_m", NH - 1)], writes=msq_w)
            P("vector",
              lambda o: o.tensor_tensor(rows[0:1, RB + SC: RB + 2 * SC],
                                        rows[0:1, RB + SC: RB + 2 * SC], msq,
                                        ALU.subtract),
              reads=[("rows_s", 0), ("rows_s", NH - 1)] + msq_r,
              writes=[("rows_s", 0), ("rows_s", NH - 1)])
            P("scalar",
              lambda o: o.activation(rows[0:1, RB + SC: RB + 2 * SC],
                                     rows[0:1, RB + SC: RB + 2 * SC],
                                     AF.Ln, bias=epsc_ap),
              reads=[("rows_s", 0), ("epsc",)],
              writes=[("rows_s", 0), ("rows_s", NH - 1)])
            P("scalar",
              lambda o: o.activation(rows[0:1, RB + SC: RB + 2 * SC],
                                     rows[0:1, RB + SC: RB + 2 * SC],
                                     AF.Exp, scale=-0.5),
              reads=[("rows_s", 0)], writes=[("rows_s", 0), ("rows_s", NH - 1)])
            for seg in range(NH):
                pb = FFNB[2 + seg]
                P("tensor",
                  lambda o, sg=NH + seg, p_=pb: o.matmul(
                      ps[p_][:, 0:SHW], onesr_ap,
                      rows[0:1, RB + SHW * sg: RB + SHW * (sg + 1)],
                      start=True, stop=True, skip_group_check=True),
                  reads=[("rows_s", 0), ("rows_s", NH - 1), ("onesr",)],
                  writes=[("PB", pb)], sem="S_taux")
                P("scalar",
                  lambda o, sg=seg, p_=pb: o.copy(
                      rb_b[:, SHW * sg: SHW * (sg + 1)], ps[p_][:, 0:SHW]),
                  reads=[("PB", pb)],
                  writes=[(mrb_tag + "_r",), ("rows_s", seg)])

        if LATE_CONST:
            P("vector", lambda o: o.memset(onesr_ap, 1.0),
              writes=[("onesr",), ("wts", 3), ("wts", 3, 0), ("wts", 3, 1)])
            P("vector", lambda o: o.memset(epsc_ap, 1e-5), writes=[("epsc",)])
        ln_finalize("mrb")

        def q_dma(w):
            slot = w % 2
            q0 = (D // 2) * slot
            out_ap = scr_i32[:, q0: q0 + D // 2]
            dma("scalar", "Qq%d" % slot, out_ap,
                t_pt.ap()[w].rearrange("j jt b -> j (jt b)"),
                writes=[("qsl", slot), ("msl", slot)])

        def unpack(w):
            slot = w % 2
            q0 = (D // 2) * slot
            qs = scr_i32[:, q0: q0 + D // 2]
            P("vector",
              lambda o, q=qs: o.tensor_scalar(scr_i32[:, D: D + D // 2], q, 15, None,
                                              ALU.bitwise_and),
              reads=[("qsl", slot)], writes=[("l32",), ("msl", 2)])
            P("vector",
              lambda o, q=qs: o.tensor_scalar(scr_i32[:, D + D // 2: 2 * D], q, 4, 15,
                                              ALU.logical_shift_right, ALU.bitwise_and),
              reads=[("qsl", slot)], writes=[("h32",), ("msl", 3)])
            P("scalar",
              lambda o, s=slot: o.copy(scr_bf[:, D * s: D * s + D // 2],
                                       scr_i32[:, D: D + D // 2]),
              reads=[("l32",)], writes=[("lobf", slot), ("wts", slot)])
            P("scalar",
              lambda o, s=slot: o.copy(scr_bf[:, D * s + D // 2: D * (s + 1)],
                                       scr_i32[:, D + D // 2: 2 * D]),
              reads=[("h32",)], writes=[("hibf", slot), ("wts", slot)])

        def ffn_mm(w, h, c, first, last, fut):
            slot = w % 2
            jt = c // 2
            par = c % 2
            woff = D * slot + (D // 2) * par + 128 * jt
            bk = bank_ffn(w, h)
            return P("tensor",
                     lambda o, w_=w, h_=h, c_=c, b_=bk, wo=woff: o.matmul(
                         ps[b_][:, 0:SHW], scr_bf[:, wo: wo + 128],
                         arena_b[:, SC * c_ + SHW * h_: SC * c_ + SHW * (h_ + 1)],
                         start=first, stop=last, skip_group_check=True),
                     reads=[("lobf", slot) if par == 0 else ("hibf", slot),
                            ("hT", c)],
                     writes=([("PB", bk)] if first else []),
                     inc=(1 if last else 0), future=fut)

        def epilogue(w):
            for h in range(NH):
                bk = bank_ffn(w, h)
                P("vector",
                  lambda o, w_=w, h_=h, b_=bk: o.scalar_tensor_tensor(
                      stage[:, SHW * h_: SHW * (h_ + 1)],
                      rsb[:, SHW * h_: SHW * (h_ + 1)], zp2n[:, w_: w_ + 1],
                      ps[b_][:, 0:SHW], ALU.mult, ALU.add),
                  reads=[("PB", bk), ("rsb",), ("zp2n",)], writes=[("stage", h)])
                P("scalar",
                  lambda o, w_=w, h_=h: o.activation(
                      stage[:, SHW * h_: SHW * (h_ + 1)],
                      stage[:, SHW * h_: SHW * (h_ + 1)], AF.Identity,
                      bias=bi2s[:, w_: w_ + 1], scale=sc2s[:, w_: w_ + 1]),
                  reads=[("stage", h), ("bi2s",), ("sc2s",)], writes=[("stage", h)])
                P("vector",
                  lambda o, w_=w, h_=h: o.tensor_tensor(
                      arena_a[:, SC * w_ + SHW * h_: SC * w_ + SHW * (h_ + 1)],
                      stage[:, SHW * h_: SHW * (h_ + 1)],
                      arena_b[:, SC * w_ + SHW * h_: SC * w_ + SHW * (h_ + 1)],
                      ALU.add),
                  reads=[("stage", h), ("hT", w)],
                  writes=[("x2", w, h), ("xq", w // XDIV)])
            P("scalar",
              lambda o, w_=w: o.activation(sqst[:, 0:SC],
                                           arena_a[:, SC * w_: SC * (w_ + 1)],
                                           AF.Square),
              reads=[("x2", w, 0), ("x2", w, NH - 1)],
              writes=[("sq", 0), ("mrb_m",)])

        q_dma(0)
        if C > 1:
            q_dma(1)
        unpack(0)
        futs0 = [sch.plan("S_tensor", ahead=1 + h) for h in range(NH)]
        for c in range(C):
            eng = "gpsimd" if (c % 3 == 2 and C > 4) else "vector"
            if eng == "vector":
                stg = stage[:, 0:SC]
                stgr = [("stage", 0), ("stage", NH - 1)]
            else:
                stg = fout[:, 4 * SC: 5 * SC]
                stgr = [("pst",)]
            P(eng,
              lambda o, c_=c, sg=stg: o.tensor_tensor(
                  sg, arena_b[:, SC * c_: SC * (c_ + 1)],
                  mb_b[:], ALU.subtract),
              reads=[("h", c, 0), ("h", c, NH - 1), ("mrb_m",)],
              writes=stgr +
                     ([("EM", m) for m in range(NSL)] if c == 0 else []) +
                     ([("wts", 2), ("wts", 2, 0), ("wts", 2, 1)]
                      if eng == "gpsimd" and c == 3 else []))
            P(eng,
              lambda o, c_=c, sg=stg: o.tensor_tensor(
                  arena_b[:, SC * c_: SC * (c_ + 1)],
                  sg, rb_b[:], ALU.mult),
              reads=stgr + [("mrb_r",)],
              writes=[("hT", c), ("h", c, 0), ("h", c, NH - 1)])
            for h in range(NH):
                ffn_mm(0, h, c, c == 0, c == C - 1, futs0[h])
            for h in range(NH):
                P("tensor",
                  lambda o, c_=c, h_=h: o.matmul(
                      ps[3 + h_][0:1, 0:SHW], onesc[:, 0:1],
                      arena_b[:, SC * c_ + SHW * h_: SC * c_ + SHW * (h_ + 1)],
                      start=(c_ == 0), stop=(c_ == C - 1), skip_group_check=True),
                  reads=[("hT", c), ("ones",)], writes=[("PB", 3 + h)],
                  sem="S_taux")
        for h in range(NH):
            P("vector",
              lambda o, h_=h: o.tensor_copy(rows[0:1, RB + SHW * h_: RB + SHW * (h_ + 1)],
                                            ps[3 + h_][0:1, 0:SHW]),
              reads=[("PB", 3 + h)], writes=[("rows_m", h), ("mrb",)])
        for seg in range(NH):
            pb = (2, 7)[seg % 2]
            P("tensor",
              lambda o, sg=seg, p_=pb: o.matmul(
                  ps[p_][:, 0:SHW], onesr_ap,
                  rows[0:1, RB + SHW * sg: RB + SHW * (sg + 1)],
                  start=True, stop=True, skip_group_check=True),
              reads=[("rows_m", 0), ("rows_m", NH - 1), ("onesr",)],
              writes=[("PB", pb)], sem="S_taux")
            P("scalar",
              lambda o, sg=seg, p_=pb: o.copy(
                  rsb[:, SHW * sg: SHW * (sg + 1)], ps[p_][:, 0:SHW]),
              reads=[("PB", pb)], writes=[("rsb",)])
        unpack(1)
        epilogue(0)

        for w in range(1, C):
            if w + 1 < C:
                q_dma(w + 1)
            if w + 1 < C:
                unpack(w + 1)
            corder = list(range(0, C, 2)) + list(range(1, C, 2))
            for h in range(NH):
                fut = sch.plan("S_tensor")
                for ci, c in enumerate(corder):
                    ffn_mm(w, h, c, ci == 0, ci == C - 1, fut)
            stats_mms(w - 1, arena_a, lambda vv, hh: ("x2", vv, hh), ("sq", 0),
                      w - 1 == 0, w - 1 == C - 1)
            epilogue(w)
        stats_mms(C - 1, arena_a, lambda vv, hh: ("x2", vv, hh), ("sq", 0),
                  C - 1 == 0, True)

        ln_finalize("mrb2")
        for c in range(C):
            eng = "gpsimd" if ((3 * c) % 7 < 3 and C > 4) else "vector"
            if eng == "vector":
                stg = stage[:, 0:SC]
                stgr = [("stage", 0), ("stage", NH - 1)]
            else:
                stg = fout[:, 4 * SC: 5 * SC]
                stgr = [("pst",)]
            P(eng,
              lambda o, c_=c, sg=stg: o.tensor_tensor(
                  sg, arena_a[:, SC * c_: SC * (c_ + 1)],
                  mb_b[:], ALU.subtract),
              reads=[("x2", c, 0), ("x2", c, NH - 1), ("mrb2_m",)],
              writes=stgr)
            P(eng,
              lambda o, c_=c, sg=stg: o.tensor_tensor(
                  fout[:, SC * (c_ % 2): SC * (c_ % 2 + 1)],
                  sg, rb_b[:], ALU.mult),
              reads=stgr + [("mrb2_r",)],
              writes=[("fo", c % 2), ("lobf", 0) if c % 2 == 0 else ("hibf", 0)])
            dma("scalar", "Qo%d" % (c % 2), t_out.ap()[128 * c: 128 * (c + 1), :],
                fout[:, SC * (c % 2): SC * (c % 2 + 1)],
                reads=[("fo", c % 2)], writes=[("outd", c)])
        P("sync", lambda o: o.nop(),
          extra=[("Qo0", sch.cnt.get("Qo0", 0)), ("Qo1", sch.cnt.get("Qo1", 0))],
          inc=0)

        @blk.sync
        def _(e):
            sch.emit_engine("sync", e, sems)

        @blk.gpsimd
        def _(e):
            sch.emit_engine("gpsimd", e, sems)

        @blk.vector
        def _(e):
            sch.emit_engine("vector", e, sems)

        @blk.scalar
        def _(e):
            sch.emit_engine("scalar", e, sems)

        @blk.tensor
        def _(e):
            sch.emit_engine("tensor", e, sems)

    nc._t_out = t_out
    return nc


def host_prepare(D, SC_total, x, W, mask, packed, scale, zp, bias, n_cores):
    pi = _perm(D)
    C = D // 128
    T = x.shape[0]
    SC = T // n_cores
    xT = np.ascontiguousarray(x.T)[pi]
    wtp = W[np.ix_(pi, pi)].T
    wtpr = np.ascontiguousarray(
        wtp.reshape(C, 128, C, 128).transpose(2, 1, 0, 3))
    P = packed.reshape(D, D // 2)
    ptp = P[pi].T
    ptpr = np.ascontiguousarray(
        ptp.reshape(C // 2, 128, C, 128).transpose(2, 1, 0, 3))
    zp2 = np.ascontiguousarray(zp.reshape(D)[pi].reshape(C, 128).T)
    sc2 = np.ascontiguousarray(scale.reshape(D)[pi].reshape(C, 128).T)
    bi2 = np.ascontiguousarray(bias.reshape(D)[pi].reshape(C, 128).T)
    ar = np.arange(128)
    g8a = (ar[:, None] // 32 == np.arange(8)[None, :]).astype(BFNP)
    g8b = (ar[:, None] // 32 + 4 == np.arange(8)[None, :]).astype(BFNP)
    r32 = np.arange(32)
    sel = np.concatenate(
        [(r32[:, None] == 8 * q + ar[None, :] // 16) for q in range(4)],
        axis=1).astype(BFNP)
    kbn = D // 32
    maskn = np.ascontiguousarray(
        mask.astype(np.int32, copy=False).reshape(D, kbn, 32)
        .transpose(0, 2, 1).reshape(D, D))
    shared = dict(wtpr=wtpr.astype(np.float32, copy=False), maskn=maskn, sel=sel,
                  ptpr=ptpr.astype(np.int32, copy=False),
                  zp2=zp2.astype(np.float32, copy=False),
                  sc2=sc2.astype(np.float32, copy=False),
                  bi2=bi2.astype(np.float32, copy=False), g8a=g8a, g8b=g8b)
    in_maps = []
    for cix in range(n_cores):
        m = dict(shared)
        m["xT"] = np.ascontiguousarray(xT[:, SC * cix: SC * (cix + 1)])
        in_maps.append(m)
    inv = np.empty(D, np.int64)
    inv[pi] = np.arange(D)
    return in_maps, inv, SC


def kernel(L0_attn_data, L0_attn_weight, L0_attn_mask, L0_ffn_weight_packed,
           L0_ffn_scale, L0_ffn_zero_point, L0_ffn_bias):
    D = L0_attn_weight.shape[0]
    B, S, _ = L0_attn_data.shape
    x = np.ascontiguousarray(np.asarray(L0_attn_data, dtype=np.float32).reshape(B * S, D))
    in_maps, inv, SC = host_prepare(
        D, B * S, x, np.asarray(L0_attn_weight, np.float32),
        np.asarray(L0_attn_mask), np.asarray(L0_ffn_weight_packed, np.int32),
        np.asarray(L0_ffn_scale, np.float32), np.asarray(L0_ffn_zero_point, np.float32),
        np.asarray(L0_ffn_bias, np.float32), N_CORES)
    nc = build_program(D, SC)
    res = run_bass_kernel_spmd(nc, in_maps, core_ids=list(range(N_CORES)))
    out = np.empty((B * S, D), np.float32)
    for cix in range(N_CORES):
        outT = np.asarray(res.results[cix]["outT"])
        out[SC * cix: SC * (cix + 1)] = outT.T[:, inv]
    return out.reshape(B, S, D)


# version 45
# speedup vs baseline: 1.2132x; 1.0530x over previous
import numpy as np
import ml_dtypes
from contextlib import ExitStack

import concourse.bass as bass
import concourse.mybir as mybir
from concourse.bass_utils import run_bass_kernel_spmd

F32 = mybir.dt.float32
BF16 = mybir.dt.bfloat16
I32 = mybir.dt.int32
AF = mybir.ActivationFunctionType
ALU = mybir.AluOpType
AX = mybir.AxisListType
BFNP = ml_dtypes.bfloat16

N_CORES = 8


def _perm(d):
    pi = np.empty(d, np.int64)
    ar = np.arange(128)
    for t in range(d // 256):
        b = 256 * t
        pi[b:b + 128] = b + 2 * ar
        pi[b + 128:b + 256] = b + 1 + 2 * ar
    return pi


class Sched:
    ENGINES = ("sync", "gpsimd", "vector", "scalar", "tensor")

    def __init__(self):
        self.ev = {e: [] for e in self.ENGINES}
        self.cnt = {}
        self.state = {}

    def plan(self, sem, ahead=1):
        return (sem, self.cnt.get(sem, 0) + ahead)

    def op(self, eng, fn, reads=(), writes=(), sem=None, inc=1, extra=(),
           future=None, record=None):
        sem = sem or ("S_" + eng)
        waits = {}

        def need(tok):
            if tok is None:
                return
            s, v = tok
            if waits.get(s, 0) < v:
                waits[s] = v

        for r in reads:
            st = self.state.get(r)
            if st:
                need(st[0])
        for wr in writes:
            st = self.state.get(wr)
            if st:
                need(st[0])
                for s, v in st[1].items():
                    need((s, v))
        for tok in extra:
            need(tok)
        if inc:
            newv = self.cnt.get(sem, 0) + inc
            self.cnt[sem] = newv
            tok = (sem, newv)
        else:
            tok = future
        if record is not None:
            tok = record
        self.ev[eng].append((waits, fn, (sem, inc) if inc else None))
        for r in reads:
            st = self.state.setdefault(r, [None, {}])
            if tok is not None:
                s, v = tok
                st[1][s] = max(st[1].get(s, 0), v)
        for wr in writes:
            self.state[wr] = [tok, {}]
        return tok

    def emit_engine(self, e, obj, sems):
        hw = {}
        for waits, fn, inc in self.ev[e]:
            for s in sorted(waits):
                v = waits[s]
                if hw.get(s, 0) >= v:
                    continue
                obj.wait_ge(sems[s], v)
                hw[s] = v
            ins = fn(obj)
            if inc is not None:
                s, n = inc
                ins.then_inc(sems[s], n)


def build_program(D, SC):
    C = D // 128
    NSL = D // 256
    XG = C // 4
    SHW = min(512, SC)
    NH = SC // SHW
    KBN = D // 32
    KBT = max(32, KBN)
    EMC = (C // 2) * KBN
    assert SC % SHW == 0 and NH in (1, 2)

    nc = bass.Bass("TRN2", target_bir_lowering=False, debug=False,
                   dynamic_dma_scratch_size=16384)

    t_xT = nc.dram_tensor("xT", [D, SC], F32, kind="ExternalInput")
    t_wt = nc.dram_tensor("wtpr", [C, 128, C, 128], F32, kind="ExternalInput")
    t_mask = nc.dram_tensor("maskn", [D, D], I32, kind="ExternalInput")
    t_pt = nc.dram_tensor("ptpr", [C, 128, C // 2, 128], I32, kind="ExternalInput")
    t_zp2 = nc.dram_tensor("zp2", [128, C], F32, kind="ExternalInput")
    t_sc2 = nc.dram_tensor("sc2", [128, C], F32, kind="ExternalInput")
    t_bi2 = nc.dram_tensor("bi2", [128, C], F32, kind="ExternalInput")
    t_g8a = nc.dram_tensor("g8a", [128, 8], BF16, kind="ExternalInput")
    t_g8b = nc.dram_tensor("g8b", [128, 8], BF16, kind="ExternalInput")
    t_sel = nc.dram_tensor("sel", [32, 512], BF16, kind="ExternalInput")
    t_out = nc.dram_tensor("outT", [D, SC], F32, kind="ExternalOutput")
    t_bmd = nc.dram_tensor("bm_scr", [KBN, KBN], BF16, kind="Internal")
    t_bmx = nc.dram_tensor("bmx_scr", [16 * KBN, KBN], BF16, kind="Internal")
    t_rowd = nc.dram_tensor("row_scr", [1, 2 * SC], F32, kind="Internal")
    t_rsd = nc.dram_tensor("rs_scr", [1, SC], F32, kind="Internal")

    sch = Sched()
    P = sch.op

    with ExitStack() as st_:
        E = st_.enter_context
        arena_a = E(nc.sbuf_tensor("arena_a", [128, C * SC], BF16))
        arena_b = E(nc.sbuf_tensor("arena_b", [128, C * SC], BF16))
        scr_i32 = E(nc.sbuf_tensor("scr_i32", [128, 2 * D], I32))
        mskv = scr_i32.bitcast(BF16)
        scr_bf = E(nc.sbuf_tensor("scr_bf", [128, 4 * D], BF16))
        em = E(nc.sbuf_tensor("em", [128, EMC], BF16))
        crbf = E(nc.sbuf_tensor("crbf", [128, 4 * KBN], BF16))
        bmb = E(nc.sbuf_tensor("bmb", [32, KBT], BF16))
        bmtr = E(nc.sbuf_tensor("bmtr", [32, KBT], BF16))
        onesc = E(nc.sbuf_tensor("onesc", [128, 1], BF16))
        g8a = E(nc.sbuf_tensor("g8a_s", [128, 8], BF16))
        g8b = E(nc.sbuf_tensor("g8b_s", [128, 8], BF16))
        sel_s = E(nc.sbuf_tensor("sel_s", [32, 512], BF16))
        zp2n = E(nc.sbuf_tensor("zp2n", [128, C], BF16))
        sc2s = E(nc.sbuf_tensor("sc2s", [128, C], F32))
        bi2s = E(nc.sbuf_tensor("bi2s", [128, C], BF16))
        mb_b = E(nc.sbuf_tensor("mb_b", [128, SC], BF16))
        rb_b = E(nc.sbuf_tensor("rb_b", [128, SC], F32))
        rsb = E(nc.sbuf_tensor("rsb", [128, SC], BF16))
        sqst = mb_b
        if EMC * 2 == SC * 4:
            stage = em.bitcast(F32)
            stage_b = em
        else:
            stage = E(nc.sbuf_tensor("stage", [128, SC], F32))
            stage_b = stage.bitcast(BF16)
        if 7 * SC + 256 <= 2 * D:
            fout = scr_bf.bitcast(F32)
            rows = fout
            RB = 5 * SC

            onesr_ap = fout[0:1, RB + 2 * SC: RB + 2 * SC + 128]
            epsc_ap = fout[0:1, RB + 2 * SC + 128: RB + 2 * SC + 129]
            LATE_CONST = True
        else:
            fout = E(nc.sbuf_tensor("fout", [128, 2 * SC], F32))
            rowsmall = E(nc.sbuf_tensor("rowsmall", [1, 2 * SC], F32))
            rows = rowsmall
            RB = 0
            onesr_t = E(nc.sbuf_tensor("onesr", [1, 128], F32))
            epsc_t = E(nc.sbuf_tensor("epsc", [1, 1], F32))
            onesr_ap = onesr_t[0:1, :]
            epsc_ap = epsc_t[0:1, :]
            LATE_CONST = False
        ps = [E(nc.psum_tensor("ps%d" % i, [128, SHW], F32)) for i in range(8)]
        sems = {k: E(nc.semaphore(k)) for k in
                ("S_sync", "S_gpsimd", "S_vector", "S_scalar", "S_tensor",
                 "S_taux", "Q_sm", "Q_sg", "Q_b", "Q_r",
                 "Q_x0", "Q_x1", "Q_x2", "Q_x3",
                 "Qm0", "Qm1", "Qm2", "Qm3", "Qsc0", "Qsc1", "Qem0", "Qem1", "Qem2", "Qem3", "Qw0", "Qw1", "Qw2", "Qw3", "Qq0", "Qq1", "Qo0", "Qo1")}
        blk = E(nc.Block())

        def dma(eng, qsem, out_ap, in_ap, reads=(), writes=(), record=None,
                selfwait=False):
            tok = P(eng, lambda o, a=out_ap, b=in_ap: o.dma_start(out=a, in_=b),
                    reads=reads, writes=writes, sem=qsem, inc=16, record=record)
            if selfwait:
                P(eng, lambda o: o.nop(), extra=[(qsem, sch.cnt[qsem])], inc=0)
            return tok

        def bank_data(vw, h):
            return (NH * vw + h) % 3

        FFNB = (0, 1, 2, 7)

        def bank_ffn(w, h):
            return FFNB[(NH * w + h) % 4]

        smtot = ("Q_sm", 4 * 16)
        sgtot = ("Q_sg", 2 * 16)
        dma("gpsimd", "Q_sg", zp2n[:], t_zp2.ap(), writes=[("zp2s",)], record=sgtot)
        dma("sync", "Q_sm", sc2s[:], t_sc2.ap(), writes=[("sc2s",)], record=smtot)
        dma("gpsimd", "Q_sg", bi2s[:], t_bi2.ap(), writes=[("bi2s",)], record=sgtot)
        dma("sync", "Q_sm", g8a[:], t_g8a.ap(), writes=[("g8a",)], record=smtot)
        dma("sync", "Q_sm", g8b[:], t_g8b.ap(), writes=[("g8b",)], record=smtot)
        dma("sync", "Q_sm", sel_s[:], t_sel.ap(), writes=[("sel",)], record=smtot)
        P("vector", lambda o: o.memset(onesc[:], 1.0), writes=[("ones",)])
        if not LATE_CONST:
            P("vector", lambda o: o.memset(onesr_ap, 1.0), writes=[("onesr",)])
            P("vector", lambda o: o.memset(epsc_ap, 1e-5), writes=[("epsc",)])
        P("vector", lambda o: o.memset(bmb[:], 0.0), writes=[("bmb",)])
        P("vector", lambda o: o.memset(bmtr[:], 0.0), writes=[("bmtr",)])
        P("vector", lambda o: o.tensor_scalar(zp2n[:], zp2n[:], -1.0, None, ALU.mult),
          reads=[("zp2s",)], writes=[("zp2n",)])

        NXD = min(4, XG)
        XDIV = C // NXD
        xT_r = t_xT.ap().rearrange("(g i p) s -> g p i s", i=XDIV, p=128)

        def xq_load(g):
            out_ap = arena_a[:, XDIV * SC * g: XDIV * SC * (g + 1)].rearrange(
                "p (i s) -> p i s", s=SC)
            dma("gpsimd", "Q_x%d" % g, out_ap, xT_r[g], writes=[("xq", g)])

        def mask_stage_a(m, dmas_only=False, reds_only=False):
            for half in range(2):
                i = 2 * m + half
                slot = i % 4
                cslot = i % 4
                if not reds_only:
                    msl = mskv[:, D * slot: D * (slot + 1)]
                    dma("gpsimd", "Qm%d" % slot, msl,
                        t_mask.ap()[128 * i: 128 * (i + 1), :],
                        writes=[("msl", slot)])
                if dmas_only:
                    continue
                def red_fn(o, d=cslot, sl=slot):
                    mv = bass.AP(mskv, D * sl, [[4 * D, 128], [32, KBN], [1, 32]])
                    with nc.allow_low_precision(reason="int 0/1 sums <= 1024 exact in bf16"):
                        return o.tensor_reduce(crbf[:, KBN * d: KBN * (d + 1)],
                                               mv, AX.X, ALU.add)
                P("vector", red_fn, reads=[("msl", slot)], writes=[("crb", cslot)])

        def mask_stage_b(m):
            for half in range(2):
                i = 2 * m + half
                cslot = i % 4
                gg = g8a if half == 0 else g8b
                ggr = ("g8a",) if half == 0 else ("g8b",)
                P("tensor",
                  lambda o, d=cslot, g_=gg, h=half: o.matmul(
                      ps[7][0:8, 0:KBN], g_[:, 0:8],
                      crbf[:, KBN * d: KBN * (d + 1)],
                      start=(h == 0), stop=(h == 1), skip_group_check=True),
                  reads=[("crb", cslot), ggr], writes=[("PB", 7)], sem="S_taux")
            P("vector",
              lambda o: o.tensor_scalar(bmb[0:8, 0:KBN], ps[7][0:8, 0:KBN],
                                        0.0, None, ALU.is_gt),
              reads=[("PB", 7)], writes=[("bmb",)])
            P("vector", lambda o: o.transpose(bmtr[:, 0:KBT], bmb[:, 0:KBT]),
              reads=[("bmb",)], writes=[("bmtr",)])
            rcnt = min(32, KBN)
            jcnt = (KBN + 31) // 32
            qs = "Qsc%d" % (m % 2)
            sct = (qs, sch.cnt.get(qs, 0) + 16 * jcnt)
            for j in range(jcnt):
                in_ap = bass.AP(bmtr, 32 * j, [[KBT, rcnt], [0, 16], [1, 8]])
                out_ap = bass.AP(t_bmx, (16 * 32 * j) * KBN + 8 * m,
                                 [[16 * KBN, rcnt], [KBN, 16], [1, 8]])
                dma("sync", qs, out_ap, in_ap, reads=[("bmtr",)],
                    writes=([("bmx", m)] if j == 0 else []), record=sct)
            in2 = bass.AP(t_bmx, 8 * m, [[KBN, 128], [128 * KBN, C // 2], [1, 8]])
            out2 = bass.AP(em, 8 * m, [[EMC, 128], [KBN, C // 2], [1, 8]])
            dma("sync", "Qem%d" % (m % 4), out2, in2, reads=[("bmx", m)],
                writes=[("EM", m)])

        def wt_refill(v):
            slot = v % 4
            out_ap = scr_bf[:, D * slot: D * (slot + 1)]
            in_ap = t_wt.ap()[v].rearrange("k c n -> k (c n)")
            dma("gpsimd", "Qw%d" % slot, out_ap, in_ap,
                writes=[("wts", slot), ("wts", slot, 0), ("wts", slot, 1)])

        def wt_mult(v):
            slot = v % 4
            off = D * slot
            for par in range(2):
                in1 = bass.AP(em, 8 * (v // 2),
                              [[EMC, 128], [KBN, C // 2], [1, 8], [0, 16]])
                io = bass.AP(scr_bf, off + 128 * par,
                             [[4 * D, 128], [256, C // 2], [16, 8], [1, 16]])
                P("vector" if par == 0 else "gpsimd",
                  lambda o, a=io, b=in1: o.tensor_tensor(a, a, b, ALU.mult),
                  reads=[("wts", slot), ("EM", v // 2)],
                  writes=[("wts", slot, par)])

        wt_refill(0)
        mask_stage_a(0)
        xq_load(0)
        if C > 1:
            wt_refill(1)
        if NSL > 1:
            mask_stage_a(1, dmas_only=True)
        if C > 2:
            wt_refill(2)
        for g_ in range(1, NXD):
            xq_load(g_)
        mask_stage_b(0)
        wt_mult(0)
        if NSL > 1:
            mask_stage_a(1, reds_only=True)

        def stats_mms(vv, src, s1_regions, sq_region, first, last):
            for h in range(NH):
                P("tensor",
                  lambda o, v_=vv, h_=h, s_=src: o.matmul(
                      ps[3 + h_][0:1, 0:SHW], onesc[:, 0:1],
                      s_[:, SC * v_ + SHW * h_: SC * v_ + SHW * (h_ + 1)],
                      start=first, stop=last, skip_group_check=True),
                  reads=[s1_regions(vv, h), ("ones",)], writes=[("PB", 3 + h)],
                  sem="S_taux")
            for h in range(NH):
                P("tensor",
                  lambda o, h_=h, v_=vv: o.matmul(
                      ps[5 + h_][0:1, 0:SHW], onesc[:, 0:1],
                      sqst[:, SHW * h_: SHW * (h_ + 1)],
                      start=first, stop=last, skip_group_check=True),
                  reads=[sq_region, ("ones",)], writes=[("PB", 5 + h)],
                  sem="S_taux")

        for v in range(C):
            if v + 1 < C:
                wt_mult(v + 1)
            if v % 2 == 1:
                if (v + 3) // 2 < NSL:
                    mask_stage_b((v + 3) // 2)
            corder = list(range(0, C, 2)) + list(range(1, C, 2))
            for h in range(NH):
                bk = bank_data(v, h)
                fut = sch.plan("S_tensor")
                for ci, c in enumerate(corder):
                    P("tensor",
                      lambda o, v_=v, h_=h, c_=c, b_=bk, st=(ci == 0),
                      sp=(ci == C - 1): o.matmul(
                          ps[b_][:, 0:SHW],
                          scr_bf[:, D * (v_ % 4) + 128 * c_: D * (v_ % 4) + 128 * (c_ + 1)],
                          arena_a[:, SC * c_ + SHW * h_: SC * c_ + SHW * (h_ + 1)],
                          start=st, stop=sp,
                          skip_group_check=True),
                      reads=[("wts", v % 4, c % 2), ("xq", c // XDIV)],
                      writes=([("PB", bk)] if ci == 0 else []),
                      inc=(1 if ci == C - 1 else 0), future=fut)
            if v >= 1:
                stats_mms(v - 1, arena_b, lambda vv, hh: ("h", vv, hh), ("sq", 0),
                          v - 1 == 0, v - 1 == C - 1)
            for h in range(NH):
                bk = bank_data(v, h)
                P("vector",
                  lambda o, v_=v, h_=h, b_=bk: o.tensor_tensor(
                      arena_b[:, SC * v_ + SHW * h_: SC * v_ + SHW * (h_ + 1)],
                      ps[b_][:, 0:SHW],
                      arena_a[:, SC * v_ + SHW * h_: SC * v_ + SHW * (h_ + 1)],
                      ALU.add),
                  reads=[("PB", bk), ("xq", v // XDIV)], writes=[("h", v, h)])
            P("scalar",
              lambda o, v_=v: o.activation(sqst[:, 0:SC],
                                           arena_b[:, SC * v_: SC * (v_ + 1)],
                                           AF.Square),
              reads=[("h", v, 0), ("h", v, NH - 1)], writes=[("sq", 0)])
            if v == 0 and NSL > 2:
                mask_stage_a(2)
            if v == 0 and NSL > 1:
                mask_stage_b(1)
            if v % 2 == 0 and v // 2 + 3 < NSL:
                mask_stage_a(v // 2 + 3)
            if v + 3 < C:
                wt_refill(v + 3)
        stats_mms(C - 1, arena_b, lambda vv, hh: ("h", vv, hh), ("sq", 0),
                  C - 1 == 0, True)

        def ln_finalize(mrb_tag):
            for h in range(NH):
                P("vector",
                  lambda o, h_=h: o.tensor_scalar(
                      rows[0:1, RB + SHW * h_: RB + SHW * (h_ + 1)], ps[3 + h_][0:1, 0:SHW],
                      1.0 / D, None, ALU.mult),
                  reads=[("PB", 3 + h)],
                  writes=[("rows_m", h), (mrb_tag + "_m",)])
            for seg in range(NH):
                pb = FFNB[seg]
                P("tensor",
                  lambda o, sg=seg, p_=pb: o.matmul(
                      ps[p_][:, 0:SHW], onesr_ap,
                      rows[0:1, RB + SHW * sg: RB + SHW * (sg + 1)],
                      start=True, stop=True, skip_group_check=True),
                  reads=[("rows_m", seg), ("onesr",)],
                  writes=[("PB", pb)], sem="S_taux")
                P("scalar",
                  lambda o, sg=seg, p_=pb: o.copy(
                      mb_b[:, SHW * sg: SHW * (sg + 1)], ps[p_][:, 0:SHW]),
                  reads=[("PB", pb)],
                  writes=[(mrb_tag + "_m",), ("rows_m", seg), ("sq", 0)])
            for h in range(NH):
                P("vector",
                  lambda o, h_=h: o.tensor_scalar(
                      rows[0:1, RB + SC + SHW * h_: RB + SC + SHW * (h_ + 1)],
                      ps[5 + h_][0:1, 0:SHW], 1.0 / D, None, ALU.mult),
                  reads=[("PB", 5 + h)],
                  writes=[("rows_s", h), (mrb_tag + "_r",)])
            if 2 * SC <= D:
                msq = fout[0:1, 4 * SC: 5 * SC]
                msq_w = [("pst",)]
                msq_r = [("pst",)]
            else:
                msq = stage[0:1, 0:SC]
                msq_w = [("pst",), ("stage", 0), ("stage", NH - 1)]
                msq_r = list(msq_w)
            P("vector",
              lambda o: o.tensor_tensor(msq, rows[0:1, RB: RB + SC],
                                        rows[0:1, RB: RB + SC], ALU.mult),
              reads=[("rows_m", 0), ("rows_m", NH - 1)], writes=msq_w)
            P("vector",
              lambda o: o.tensor_tensor(rows[0:1, RB + SC: RB + 2 * SC],
                                        rows[0:1, RB + SC: RB + 2 * SC], msq,
                                        ALU.subtract),
              reads=[("rows_s", 0), ("rows_s", NH - 1)] + msq_r,
              writes=[("rows_s", 0), ("rows_s", NH - 1)])
            P("scalar",
              lambda o: o.activation(rows[0:1, RB + SC: RB + 2 * SC],
                                     rows[0:1, RB + SC: RB + 2 * SC],
                                     AF.Ln, bias=epsc_ap),
              reads=[("rows_s", 0), ("epsc",)],
              writes=[("rows_s", 0), ("rows_s", NH - 1)])
            P("scalar",
              lambda o: o.activation(rows[0:1, RB + SC: RB + 2 * SC],
                                     rows[0:1, RB + SC: RB + 2 * SC],
                                     AF.Exp, scale=-0.5),
              reads=[("rows_s", 0)], writes=[("rows_s", 0), ("rows_s", NH - 1)])
            for seg in range(NH):
                pb = FFNB[2 + seg]
                P("tensor",
                  lambda o, sg=NH + seg, p_=pb: o.matmul(
                      ps[p_][:, 0:SHW], onesr_ap,
                      rows[0:1, RB + SHW * sg: RB + SHW * (sg + 1)],
                      start=True, stop=True, skip_group_check=True),
                  reads=[("rows_s", 0), ("rows_s", NH - 1), ("onesr",)],
                  writes=[("PB", pb)], sem="S_taux")
                P("scalar",
                  lambda o, sg=seg, p_=pb: o.copy(
                      rb_b[:, SHW * sg: SHW * (sg + 1)], ps[p_][:, 0:SHW]),
                  reads=[("PB", pb)],
                  writes=[(mrb_tag + "_r",), ("rows_s", seg)])

        if LATE_CONST:
            P("vector", lambda o: o.memset(onesr_ap, 1.0),
              writes=[("onesr",), ("wts", 3), ("wts", 3, 0), ("wts", 3, 1)])
            P("vector", lambda o: o.memset(epsc_ap, 1e-5), writes=[("epsc",)])
        ln_finalize("mrb")

        def q_dma(w):
            slot = w % 2
            q0 = (D // 2) * slot
            out_ap = scr_i32[:, q0: q0 + D // 2]
            dma("scalar", "Qq%d" % slot, out_ap,
                t_pt.ap()[w].rearrange("j jt b -> j (jt b)"),
                writes=[("qsl", slot), ("msl", slot)])

        def unpack(w):
            slot = w % 2
            q0 = (D // 2) * slot
            qs = scr_i32[:, q0: q0 + D // 2]
            P("vector",
              lambda o, q=qs: o.tensor_scalar(scr_i32[:, D: D + D // 2], q, 15, None,
                                              ALU.bitwise_and),
              reads=[("qsl", slot)], writes=[("l32",), ("msl", 2)])
            P("vector",
              lambda o, q=qs: o.tensor_scalar(scr_i32[:, D + D // 2: 2 * D], q, 4, 15,
                                              ALU.logical_shift_right, ALU.bitwise_and),
              reads=[("qsl", slot)], writes=[("h32",), ("msl", 3)])
            P("scalar",
              lambda o, s=slot: o.copy(scr_bf[:, D * s: D * s + D // 2],
                                       scr_i32[:, D: D + D // 2]),
              reads=[("l32",)], writes=[("lobf", slot), ("wts", slot)])
            P("scalar",
              lambda o, s=slot: o.copy(scr_bf[:, D * s + D // 2: D * (s + 1)],
                                       scr_i32[:, D + D // 2: 2 * D]),
              reads=[("h32",)], writes=[("hibf", slot), ("wts", slot)])

        def ffn_mm(w, h, c, first, last, fut):
            slot = w % 2
            jt = c // 2
            par = c % 2
            woff = D * slot + (D // 2) * par + 128 * jt
            bk = bank_ffn(w, h)
            return P("tensor",
                     lambda o, w_=w, h_=h, c_=c, b_=bk, wo=woff: o.matmul(
                         ps[b_][:, 0:SHW], scr_bf[:, wo: wo + 128],
                         arena_b[:, SC * c_ + SHW * h_: SC * c_ + SHW * (h_ + 1)],
                         start=first, stop=last, skip_group_check=True),
                     reads=[("lobf", slot) if par == 0 else ("hibf", slot),
                            ("hT", c)],
                     writes=([("PB", bk)] if first else []),
                     inc=(1 if last else 0), future=fut)

        def epilogue(w):
            for h in range(NH):
                bk = bank_ffn(w, h)
                P("vector",
                  lambda o, w_=w, h_=h, b_=bk: o.scalar_tensor_tensor(
                      stage[:, SHW * h_: SHW * (h_ + 1)],
                      rsb[:, SHW * h_: SHW * (h_ + 1)], zp2n[:, w_: w_ + 1],
                      ps[b_][:, 0:SHW], ALU.mult, ALU.add),
                  reads=[("PB", bk), ("rsb",), ("zp2n",)], writes=[("stage", h)])
                P("scalar",
                  lambda o, w_=w, h_=h: o.activation(
                      stage[:, SHW * h_: SHW * (h_ + 1)],
                      stage[:, SHW * h_: SHW * (h_ + 1)], AF.Identity,
                      bias=bi2s[:, w_: w_ + 1], scale=sc2s[:, w_: w_ + 1]),
                  reads=[("stage", h), ("bi2s",), ("sc2s",)], writes=[("stage", h)])
                P("vector",
                  lambda o, w_=w, h_=h: o.tensor_tensor(
                      arena_a[:, SC * w_ + SHW * h_: SC * w_ + SHW * (h_ + 1)],
                      stage[:, SHW * h_: SHW * (h_ + 1)],
                      arena_b[:, SC * w_ + SHW * h_: SC * w_ + SHW * (h_ + 1)],
                      ALU.add),
                  reads=[("stage", h), ("hT", w)],
                  writes=[("x2", w, h), ("xq", w // XDIV)])
            P("scalar",
              lambda o, w_=w: o.activation(sqst[:, 0:SC],
                                           arena_a[:, SC * w_: SC * (w_ + 1)],
                                           AF.Square),
              reads=[("x2", w, 0), ("x2", w, NH - 1)],
              writes=[("sq", 0), ("mrb_m",)])

        q_dma(0)
        if C > 1:
            q_dma(1)
        unpack(0)
        futs0 = [sch.plan("S_tensor", ahead=1 + h) for h in range(NH)]
        for c in range(C):
            eng = "gpsimd" if (c % 3 == 2 and C > 4) else "vector"
            if eng == "vector":
                stg = stage[:, 0:SC]
                stgr = [("stage", 0), ("stage", NH - 1)]
            else:
                stg = fout[:, 4 * SC: 5 * SC]
                stgr = [("pst",)]
            P(eng,
              lambda o, c_=c, sg=stg: o.tensor_tensor(
                  sg, arena_b[:, SC * c_: SC * (c_ + 1)],
                  mb_b[:], ALU.subtract),
              reads=[("h", c, 0), ("h", c, NH - 1), ("mrb_m",)],
              writes=stgr +
                     ([("EM", m) for m in range(NSL)] if c == 0 else []) +
                     ([("wts", 2), ("wts", 2, 0), ("wts", 2, 1)]
                      if eng == "gpsimd" and c == 3 else []))
            P(eng,
              lambda o, c_=c, sg=stg: o.tensor_tensor(
                  arena_b[:, SC * c_: SC * (c_ + 1)],
                  sg, rb_b[:], ALU.mult),
              reads=stgr + [("mrb_r",)],
              writes=[("hT", c), ("h", c, 0), ("h", c, NH - 1)])
            for h in range(NH):
                ffn_mm(0, h, c, c == 0, c == C - 1, futs0[h])
            for h in range(NH):
                P("tensor",
                  lambda o, c_=c, h_=h: o.matmul(
                      ps[3 + h_][0:1, 0:SHW], onesc[:, 0:1],
                      arena_b[:, SC * c_ + SHW * h_: SC * c_ + SHW * (h_ + 1)],
                      start=(c_ == 0), stop=(c_ == C - 1), skip_group_check=True),
                  reads=[("hT", c), ("ones",)], writes=[("PB", 3 + h)],
                  sem="S_taux")
        for h in range(NH):
            P("vector",
              lambda o, h_=h: o.tensor_copy(rows[0:1, RB + SHW * h_: RB + SHW * (h_ + 1)],
                                            ps[3 + h_][0:1, 0:SHW]),
              reads=[("PB", 3 + h)], writes=[("rows_m", h), ("mrb",)])
        for seg in range(NH):
            pb = (2, 7)[seg % 2]
            P("tensor",
              lambda o, sg=seg, p_=pb: o.matmul(
                  ps[p_][:, 0:SHW], onesr_ap,
                  rows[0:1, RB + SHW * sg: RB + SHW * (sg + 1)],
                  start=True, stop=True, skip_group_check=True),
              reads=[("rows_m", 0), ("rows_m", NH - 1), ("onesr",)],
              writes=[("PB", pb)], sem="S_taux")
            P("scalar",
              lambda o, sg=seg, p_=pb: o.copy(
                  rsb[:, SHW * sg: SHW * (sg + 1)], ps[p_][:, 0:SHW]),
              reads=[("PB", pb)], writes=[("rsb",)])
        unpack(1)
        epilogue(0)

        for w in range(1, C):
            if w + 1 < C:
                q_dma(w + 1)
            if w + 1 < C:
                unpack(w + 1)
            corder = list(range(0, C, 2)) + list(range(1, C, 2))
            for h in range(NH):
                fut = sch.plan("S_tensor")
                for ci, c in enumerate(corder):
                    ffn_mm(w, h, c, ci == 0, ci == C - 1, fut)
            stats_mms(w - 1, arena_a, lambda vv, hh: ("x2", vv, hh), ("sq", 0),
                      w - 1 == 0, w - 1 == C - 1)
            epilogue(w)
        stats_mms(C - 1, arena_a, lambda vv, hh: ("x2", vv, hh), ("sq", 0),
                  C - 1 == 0, True)

        ln_finalize("mrb2")
        for c in range(C):
            eng = "gpsimd" if ((3 * c) % 7 < 3 and C > 4) else "vector"
            if eng == "vector":
                stg = stage[:, 0:SC]
                stgr = [("stage", 0), ("stage", NH - 1)]
            else:
                stg = fout[:, 4 * SC: 5 * SC]
                stgr = [("pst",)]
            P(eng,
              lambda o, c_=c, sg=stg: o.tensor_tensor(
                  sg, arena_a[:, SC * c_: SC * (c_ + 1)],
                  mb_b[:], ALU.subtract),
              reads=[("x2", c, 0), ("x2", c, NH - 1), ("mrb2_m",)],
              writes=stgr)
            P(eng,
              lambda o, c_=c, sg=stg: o.tensor_tensor(
                  fout[:, SC * (c_ % 2): SC * (c_ % 2 + 1)],
                  sg, rb_b[:], ALU.mult),
              reads=stgr + [("mrb2_r",)],
              writes=[("fo", c % 2), ("lobf", 0) if c % 2 == 0 else ("hibf", 0)])
            dma("scalar", "Qo%d" % (c % 2), t_out.ap()[128 * c: 128 * (c + 1), :],
                fout[:, SC * (c % 2): SC * (c % 2 + 1)],
                reads=[("fo", c % 2)], writes=[("outd", c)])
        P("sync", lambda o: o.nop(),
          extra=[("Qo0", sch.cnt.get("Qo0", 0)), ("Qo1", sch.cnt.get("Qo1", 0))],
          inc=0)

        @blk.sync
        def _(e):
            sch.emit_engine("sync", e, sems)

        @blk.gpsimd
        def _(e):
            sch.emit_engine("gpsimd", e, sems)

        @blk.vector
        def _(e):
            sch.emit_engine("vector", e, sems)

        @blk.scalar
        def _(e):
            sch.emit_engine("scalar", e, sems)

        @blk.tensor
        def _(e):
            sch.emit_engine("tensor", e, sems)

    nc._t_out = t_out
    return nc


def host_prepare(D, SC_total, x, W, mask, packed, scale, zp, bias, n_cores):
    pi = _perm(D)
    C = D // 128
    T = x.shape[0]
    SC = T // n_cores
    xT = np.ascontiguousarray(x.T)[pi]
    wtp = W[np.ix_(pi, pi)].T
    wtpr = np.ascontiguousarray(
        wtp.reshape(C, 128, C, 128).transpose(2, 1, 0, 3))
    P = packed.reshape(D, D // 2)
    ptp = P[pi].T
    ptpr = np.ascontiguousarray(
        ptp.reshape(C // 2, 128, C, 128).transpose(2, 1, 0, 3))
    zp2 = np.ascontiguousarray(zp.reshape(D)[pi].reshape(C, 128).T)
    sc2 = np.ascontiguousarray(scale.reshape(D)[pi].reshape(C, 128).T)
    bi2 = np.ascontiguousarray(bias.reshape(D)[pi].reshape(C, 128).T)
    ar = np.arange(128)
    g8a = (ar[:, None] // 32 == np.arange(8)[None, :]).astype(BFNP)
    g8b = (ar[:, None] // 32 + 4 == np.arange(8)[None, :]).astype(BFNP)
    r32 = np.arange(32)
    sel = np.concatenate(
        [(r32[:, None] == 8 * q + ar[None, :] // 16) for q in range(4)],
        axis=1).astype(BFNP)
    maskn = np.ascontiguousarray(mask.astype(np.int32, copy=False))
    shared = dict(wtpr=wtpr.astype(np.float32, copy=False), maskn=maskn, sel=sel,
                  ptpr=ptpr.astype(np.int32, copy=False),
                  zp2=zp2.astype(np.float32, copy=False),
                  sc2=sc2.astype(np.float32, copy=False),
                  bi2=bi2.astype(np.float32, copy=False), g8a=g8a, g8b=g8b)
    in_maps = []
    for cix in range(n_cores):
        m = dict(shared)
        m["xT"] = np.ascontiguousarray(xT[:, SC * cix: SC * (cix + 1)])
        in_maps.append(m)
    inv = np.empty(D, np.int64)
    inv[pi] = np.arange(D)
    return in_maps, inv, SC


def kernel(L0_attn_data, L0_attn_weight, L0_attn_mask, L0_ffn_weight_packed,
           L0_ffn_scale, L0_ffn_zero_point, L0_ffn_bias):
    D = L0_attn_weight.shape[0]
    B, S, _ = L0_attn_data.shape
    x = np.ascontiguousarray(np.asarray(L0_attn_data, dtype=np.float32).reshape(B * S, D))
    in_maps, inv, SC = host_prepare(
        D, B * S, x, np.asarray(L0_attn_weight, np.float32),
        np.asarray(L0_attn_mask), np.asarray(L0_ffn_weight_packed, np.int32),
        np.asarray(L0_ffn_scale, np.float32), np.asarray(L0_ffn_zero_point, np.float32),
        np.asarray(L0_ffn_bias, np.float32), N_CORES)
    nc = build_program(D, SC)
    res = run_bass_kernel_spmd(nc, in_maps, core_ids=list(range(N_CORES)))
    out = np.empty((B * S, D), np.float32)
    for cix in range(N_CORES):
        outT = np.asarray(res.results[cix]["outT"])
        out[SC * cix: SC * (cix + 1)] = outT.T[:, inv]
    return out.reshape(B, S, D)
